# version 17
# baseline (speedup 1.0000x reference)
import contextlib
import math
import numpy as np
import concourse.bass as bass
import concourse.mybir as mybir
from concourse.bass_utils import run_bass_kernel_spmd
from concourse.alu_op_type import AluOpType as Op

F32 = mybir.dt.float32
BF16 = mybir.dt.bfloat16
AF = mybir.ActivationFunctionType
AX = mybir.AxisListType

B, S, D = 4, 4096, 1024
NQ = 2048
H = 16
DQ = DKV = 512
FF = 4096
EPS = 1e-5
SCALE = float(np.sqrt(np.float32(H + DQ + DKV)))
INV_SCALE = 1.0 / SCALE
PI = math.pi
TWO_PI = 2 * math.pi
HALF_PI = 0.5 * math.pi
HALF_PI_F32 = float(np.float32(0.5 * math.pi))
HALF_PI_LO = HALF_PI - HALF_PI_F32
LN1E4 = 9.210340371976184

ENGS = ("sync", "tensor", "vector", "scalar", "gpsimd")


class Prog:

    def __init__(self):
        self.steps = []
        self.barrier_toks = []

    def do(self, eng, fn, deps=(), dma=None):
        t = len(self.steps)
        dd = [d for d in deps if d is not None]
        dd.extend(self.barrier_toks)
        self.steps.append((eng, fn, tuple(dd), dma))
        return t

    def barrier(self):
        last = {}
        for i, (eng, _f, _d, dma) in enumerate(self.steps):
            last[eng if dma is None else ("ch", dma)] = i
        self.barrier_toks = list(last.values())

    def emit(self, nc, gstack, block, csem):
        steps = self.steps
        need = [False] * len(steps)
        for (_e, _f, deps, _dma) in steps:
            for d in deps:
                need[d] = True
        chans = []
        for (_e, _f, _d, dma) in steps:
            if dma is not None and dma not in chans:
                chans.append(dma)
        chsem = {c: gstack.enter_context(nc.semaphore("ch_" + c)) for c in chans}
        meta = [None] * len(steps)
        ccount = {e: 0 for e in ENGS}
        chcount = {c: 0 for c in chans}
        eng_chfinal = {e: {} for e in ENGS}
        for i, (eng, _f, _deps, dma) in enumerate(steps):
            if dma is not None:
                chcount[dma] += 16
                meta[i] = ("ch", dma, chcount[dma])
                eng_chfinal[eng][dma] = chcount[dma]
            else:
                if need[i]:
                    ccount[eng] += 1
                meta[i] = ("c", eng, ccount[eng])

        def body_for(E):
            mine = [(i, steps[i]) for i in range(len(steps)) if steps[i][0] == E]

            def body(engobj):
                ensured = {}
                for i, (_e, fn, deps, dma) in mine:
                    for d in deps:
                        kind, key2, dval = meta[d]
                        key = (kind, key2)
                        if ensured.get(key, 0) >= dval:
                            continue
                        sem = csem[key2] if kind == "c" else chsem[key2]
                        engobj.wait_ge(sem, dval)
                        ensured[key] = dval
                    inst = fn(engobj)
                    if dma is not None:
                        inst.then_inc(chsem[dma], 16)
                    elif need[i]:
                        inst.then_inc(csem[E], 1)
                for c, v in eng_chfinal[E].items():
                    engobj.wait_ge(chsem[c], v)

            return body

        for E in ENGS:
            getattr(block, E)(body_for(E))


class Banks:
    def __init__(self):
        self.readers = {b: [] for b in range(8)}
        self.writer = {b: None for b in range(8)}

    def pre_write(self, b):
        r = list(self.readers[b])
        if self.writer[b] is not None:
            r.append(self.writer[b])
        return r

    def wrote(self, b, tok):
        self.writer[b] = tok
        self.readers[b] = []

    def read(self, b, tok):
        self.readers[b].append(tok)


def build_program(dump=(), cut=None):
    nc = bass.Bass("TRN2", target_bir_lowering=False)
    p = Prog()
    bk = Banks()
    LU = {}

    xv = nc.dram_tensor("xv", [S, D], F32, kind="ExternalInput").ap()
    seg = nc.dram_tensor("seg", [2], F32, kind="ExternalInput").ap()
    win = {}
    for nm, shp in [("wdq", [D, DQ]), ("wuq", [DQ, D]), ("wqr", [DQ, D]),
                    ("wkr", [D, D]), ("wdkv", [D, DKV]), ("wuk", [DKV, D]),
                    ("wuv", [DKV, D]), ("wo", [D, D]), ("wr", [8, D]),
                    ("w1", [D, FF]), ("w2", [FF, D]),
                    ("bdq", [DQ]), ("buq", [D]), ("bqr", [D]), ("bkr", [D]),
                    ("bdkv", [DKV]), ("buk", [D]), ("buv", [D]), ("bo", [D]),
                    ("b1", [FF]), ("b2", [D])]:
        win[nm] = nc.dram_tensor(nm, shp, F32, kind="ExternalInput").ap()
    out_hbm = nc.dram_tensor("out", [NQ, D], F32, kind="ExternalOutput").ap()

    dumps = {}

    def dump_t(nm, shp):
        if nm in dump:
            dumps[nm] = nc.dram_tensor("dump_" + nm, shp, F32, kind="ExternalOutput").ap()
        return dumps.get(nm)

    xnT_hbm = nc.dram_tensor("xnT_hbm", [D, 2, S], BF16, kind="Internal").ap()
    kT_hbm = nc.dram_tensor("kT_hbm", [D, S // 128, 2, 128], BF16, kind="Internal").ap()
    v_hbm = nc.dram_tensor("v_hbm", [S, 2, H * 65], BF16, kind="Internal").ap()
    qT_hbm = nc.dram_tensor("qT_hbm", [D, 2, NQ], BF16, kind="Internal").ap()
    res_hbm = nc.dram_tensor("res_hbm", [NQ, D], F32, kind="Internal").ap()

    d_res = dump_t("res", [NQ, D])
    d_negm = dump_t("negm", [1, 1])
    d_OT = dump_t("OT", [D, NQ])

    gstack = contextlib.ExitStack()

    def mk_sb(stk):
        return lambda nm, shp, dt=F32: stk.enter_context(nc.sbuf_tensor(nm, shp, dt)).ap()

    sb = mk_sb(gstack)

    ident = sb("ident", [128, 128])
    ones8 = sb("ones8", [8, 128])
    halfpi_col = sb("halfpi_col", [128, 1])
    segcol = sb("segcol", [128, 2])
    seg_sb = sb("seg_sb", [1, 2])
    negM = sb("negM", [128, 1])
    e_sb = sb("e_sb", [128, 8 * 16])
    kmax_c = sb("kmax_c", [16, 1])
    qmax_c = sb("qmax_c", [16, 1])
    t16 = sb("t16", [16, 1])
    p7_sb = sb("p7_sb", [128, 16])
    bcol = {}
    for nm, w in [("bdq", 4), ("bdkv", 4), ("buq", 8), ("buk", 8), ("b1", 32)]:
        bcol[nm] = sb(nm + "_col", [128, w])
    buqs = sb("buqs_col", [128, 8])
    ln_c = [sb(f"ln_c{i}", [128, 1]) for i in range(4)]
    ln_m = sb("ln_m", [128, 1])
    sq_scr = sb("sq_scr", [128, 1024])
    lnout = sb("lnout", [128, 1024])
    mrow = sb("mrow", [1, 64])
    xt = [sb("xt0", [128, 1024]), None]
    xn = sb("xn", [128, 1024])
    res_t = sb("res_t", [128, 1024])
    ao_t = sb("ao_t", [128, 1024])
    wscr = [sb(f"wscr{i}", [128, 1024]) for i in range(2)]

    psum = gstack.enter_context(nc.psum_tensor("pall", [128, 4096], F32)).ap()
    PB = lambda b: psum[:, 512 * b:512 * (b + 1)]

    csem = {e: gstack.enter_context(nc.semaphore("c_" + e)) for e in ENGS}

    def _finish():
        if cut is not None:
            with nc.Block() as block:
                p.emit(nc, gstack, block, csem)
            return nc
        with gstack:
            with nc.Block() as block:
                p.emit(nc, gstack, block, csem)
        return nc

    t_id0 = p.do("gpsimd", lambda e: e.memset(ident[:], 0.0))
    t_id = p.do("gpsimd", lambda e: e.affine_select(
        out=ident[:], in_=ident[:], compare_op=Op.not_equal, fill=1.0,
        base=0, pattern=[[-1, 128]], channel_multiplier=1), deps=[t_id0])
    t_ones = p.do("gpsimd", lambda e: e.memset(ones8[:], 1.0))
    t_hp0 = p.do("gpsimd", lambda e: e.memset(halfpi_col[:], HALF_PI_F32))
    t_hp = p.do("vector", lambda e: e.tensor_scalar_add(halfpi_col[:], halfpi_col[:],
                                                        HALF_PI_LO), deps=[t_hp0])
    t_kin = p.do("gpsimd", lambda e: e.memset(kmax_c[:], -3.0e38))
    t_qin = p.do("gpsimd", lambda e: e.memset(qmax_c[:], -3.0e38))
    t_e0 = p.do("gpsimd", lambda e: e.memset(e_sb[:], 0.0))
    t_es = []
    for ic in range(8):
        for sub in range(2):
            h = 2 * ic + sub
            t_es.append(p.do("gpsimd", lambda e, ic=ic, sub=sub, h=h: e.memset(
                e_sb[64 * sub:64 * sub + 64, 16 * ic + h:16 * ic + h + 1], 1.0),
                deps=[t_e0]))

    t_b = {}
    for nm in ["bdq", "bdkv", "buq", "buk", "b1"]:
        t_b[nm] = p.do("sync", lambda e, nm=nm: e.dma_start(
            out=bcol[nm][:], in_=win[nm].rearrange("(c p) -> p c", p=128),
            allow_slow_non_contiguous=True), dma="bias0")
    t_sg0 = p.do("sync", lambda e: e.dma_start(out=seg_sb[:], in_=seg.unsqueeze(0)),
                 dma="bias0")
    for nm in ["bdq", "bdkv", "buq", "buk", "b1"]:
        t_b[nm] = t_sg0
    t_sg1 = p.do("tensor", lambda e: e.matmul(
        PB(1)[:, 0:2], ones8[0:1, :], seg_sb[:], start=True, stop=True),
        deps=[t_sg0, t_ones] + bk.pre_write(1))
    bk.wrote(1, t_sg1)
    t_seg = p.do("vector", lambda e: e.tensor_copy(segcol[:], PB(1)[:, 0:2]), deps=[t_sg1])
    bk.read(1, t_seg)

    t_buqs = p.do("vector", lambda e: e.tensor_scalar_mul(buqs[:], bcol["buq"][:], INV_SCALE),
                  deps=[t_b["buq"]])

    brow = {}

    def load_brow(stk, nm, ch):
        t = mk_sb(stk)(nm + "_row", [1, D])
        brow[nm] = t
        t_b[nm] = p.do("sync", lambda e, nm=nm, t=t: e.dma_start(
            out=t[:], in_=win[nm].unsqueeze(0)), dma=ch)

    wsb = {}

    def load_w_hl(stk, nm, I, O):
        s = mk_sb(stk)
        th = s(nm + "_h", [128, (I // 128) * O], BF16)
        tl = s(nm + "_l", [128, (I // 128) * O], BF16)
        toks = []
        for ic in range(I // 128):
            par = ic % 2
            tc = p.do("gpsimd", lambda e, ic=ic, th=th, nm=nm, O=O: e.dma_start(
                out=th[:, O * ic:O * (ic + 1)], in_=win[nm][128 * ic:128 * (ic + 1), :]),
                deps=[LU.get(("wcast", par))], dma=f"wc{par}")
            tf = p.do("sync", lambda e, ic=ic, nm=nm, O=O, par=par: e.dma_start(
                out=wscr[par][:, 0:O], in_=win[nm][128 * ic:128 * (ic + 1), :]),
                deps=[LU.get(("wscr", par))], dma=f"wf{par}")
            tt = p.do("vector", lambda e, ic=ic, th=th, tl=tl, O=O, par=par:
                      e.scalar_tensor_tensor(
                          tl[:, O * ic:O * (ic + 1)], wscr[par][:, 0:O], 1.0,
                          th[:, O * ic:O * (ic + 1)], Op.mult, Op.subtract),
                      deps=[tc, tf])
            LU[("wcast", par)] = tt
            LU[("wscr", par)] = tt
            toks.append(tt)
        wsb[nm] = (th, tl, O)
        return toks

    WH = lambda nm, ic, lo, hi: wsb[nm][0][:, wsb[nm][2] * ic + lo: wsb[nm][2] * ic + hi]
    WL = lambda nm, ic, lo, hi: wsb[nm][1][:, wsb[nm][2] * ic + lo: wsb[nm][2] * ic + hi]

    def mm_group(bank, mms, extra_deps=()):
        pre = bk.pre_write(bank) + list(extra_deps)
        tok = None
        nmm = len(mms)
        for j, (lf, rf, dd, istr, osl) in enumerate(mms):
            o = PB(bank)[:, 0:512] if osl is None else osl
            tok = p.do("tensor", lambda e, lf=lf, rf=rf, o=o, j=j, istr=istr, nmm=nmm: e.matmul(
                o, lf(), rf(), start=(j == 0), stop=(j == nmm - 1),
                is_transpose=True if istr else None),
                deps=(pre if j == 0 else []) + list(dd))
        bk.wrote(bank, tok)
        return tok

    def p3(lh, ll, rh, rl, dd):
        return [(lh, rh, dd, False, None), (lh, rl, dd, False, None),
                (ll, rh, dd, False, None)]

    def ln_stats(src_tok, src_ap, dep_extra=()):
        t1 = p.do("scalar", lambda e: e.activation(
            sq_scr[:], src_ap, AF.Identity, bias=0.0, scale=1.0, accum_out=ln_c[0][:]),
            deps=[src_tok, LU.get("sq_scr")] + list(dep_extra))
        t2 = p.do("scalar", lambda e: e.activation(
            sq_scr[:], src_ap, AF.Square, bias=0.0, scale=1.0, accum_out=ln_c[2][:]),
            deps=[src_tok, t1] + list(dep_extra))
        LU["sq_scr"] = t2
        t3 = p.do("vector", lambda e: e.tensor_scalar_mul(ln_c[1][:], ln_c[0][:], 1.0 / 1024.0),
                  deps=[t1])
        t4 = p.do("vector", lambda e: e.tensor_scalar_mul(ln_c[2][:], ln_c[2][:], 1.0 / 1024.0),
                  deps=[t2])
        t5 = p.do("vector", lambda e: e.tensor_tensor(ln_m[:], ln_c[1][:], ln_c[1][:], Op.mult),
                  deps=[t3])
        t6 = p.do("vector", lambda e: e.tensor_tensor(ln_c[2][:], ln_c[2][:], ln_m[:],
                                                      Op.subtract), deps=[t4, t5])
        t7 = p.do("vector", lambda e: e.tensor_scalar_add(ln_c[2][:], ln_c[2][:], EPS), deps=[t6])
        t8 = p.do("scalar", lambda e: e.activation(ln_c[3][:], ln_c[2][:], AF.Sqrt,
                                                   bias=0.0, scale=1.0), deps=[t7])
        t9 = p.do("vector", lambda e: e.reciprocal(ln_c[3][:], ln_c[3][:]), deps=[t8])
        t10 = p.do("vector", lambda e: e.tensor_tensor(ln_m[:], ln_c[1][:], ln_c[3][:], Op.mult),
                   deps=[t9])
        t11 = p.do("vector", lambda e: e.tensor_scalar_mul(ln_m[:], ln_m[:], -1.0), deps=[t10])
        return t11

    astack = contextlib.ExitStack()
    asb = mk_sb(astack)
    xt[1] = asb("xt1", [128, 1024])
    theta_bc = asb("theta_bc", [128, 512])
    t_th0 = p.do("gpsimd", lambda e: e.iota(
        theta_bc[0:1, :], pattern=[[1, 512]], base=0, channel_multiplier=0,
        allow_small_or_imprecise_dtypes=True))
    t_th1 = p.do("scalar", lambda e: e.activation(
        theta_bc[0:1, :], theta_bc[0:1, :], AF.Exp, bias=0.0, scale=-2.0 * LN1E4),
        deps=[t_th0])
    t_th2 = p.do("vector", lambda e: e.tensor_scalar_mul(
        theta_bc[0:1, :], theta_bc[0:1, :], 1.0 / 512.0), deps=[t_th1])
    t_th3 = p.do("tensor", lambda e: e.matmul(
        PB(0)[:, 0:512], ones8[0:1, :], theta_bc[0:1, :], start=True, stop=True),
        deps=[t_ones, t_th2] + bk.pre_write(0))
    bk.wrote(0, t_th3)
    t_th = p.do("vector", lambda e: e.tensor_copy(theta_bc[:], PB(0)[:, 0:512]),
                deps=[t_th3])
    bk.read(0, t_th)
    xnTh_t = asb("xnTh_t", [128, 8 * 512], BF16)
    xnTl_t = asb("xnTl_t", [128, 8 * 512], BF16)
    ckvh_t = asb("ckvh_t", [128, 4 * 512], BF16)
    ckvl_t = asb("ckvl_t", [128, 4 * 512], BF16)
    outh_t = asb("outh_t", [128, 8 * 512], BF16)
    outl_t = asb("outl_t", [128, 8 * 512], BF16)
    big_t = asb("big_t", [128, 4096])
    vh_t = asb("vh_t", [128, 2 * 1040], BF16)
    vl_t = asb("vl_t", [128, 2 * 1040], BF16)
    cos_t = asb("cos_t", [128, 512])
    sin_t = asb("sin_t", [128, 512])
    pos_c = asb("pos_c", [128, 1])
    rr_t = asb("rr_t", [128, 512])
    k12_t = asb("k12_t", [128, 512])
    tmpa = asb("tmpa", [128, 512])
    tmpb = asb("tmpb", [128, 512])

    def cos_sin(r0):
        base = r0 % 2048
        si = r0 // 2048
        ta = p.do("gpsimd", lambda e, base=base: e.iota(
            pos_c[:], pattern=[[0, 1]], base=base, channel_multiplier=1,
            allow_small_or_imprecise_dtypes=True), deps=[LU.get("pos")])
        tb = p.do("vector", lambda e, si=si: e.tensor_tensor(
            pos_c[:], pos_c[:], segcol[:, si:si + 1], Op.add), deps=[ta, t_seg])
        tr = p.do("vector", lambda e: e.tensor_scalar(
            rr_t[:], theta_bc[:], pos_c[:], None, Op.mult), deps=[tb, t_th, LU.get("rr")])
        tk1 = p.do("vector", lambda e: e.tensor_scalar(k12_t[:], rr_t[:], PI, None, Op.is_ge),
                   deps=[tr, LU.get("k12")])
        tk1b = p.do("vector", lambda e: e.tensor_scalar_mul(k12_t[:], k12_t[:], TWO_PI),
                    deps=[tk1])
        tk1c = p.do("vector", lambda e: e.tensor_tensor(tmpa[:], rr_t[:], k12_t[:], Op.subtract),
                    deps=[tk1b, LU.get("tmpa")])
        tsin = p.do("scalar", lambda e: e.activation(sin_t[:], tmpa[:], AF.Sin,
                                                     bias=0.0, scale=1.0),
                    deps=[tk1c, LU.get("sin_w")])
        tq1 = p.do("vector", lambda e: e.tensor_scalar(k12_t[:], rr_t[:], HALF_PI, None,
                                                       Op.is_ge), deps=[tr, tk1c])
        tq2 = p.do("vector", lambda e: e.tensor_scalar(tmpb[:], rr_t[:], 5.0 * HALF_PI, None,
                                                       Op.is_ge), deps=[tr, LU.get("tmpb")])
        tq3 = p.do("vector", lambda e: e.tensor_tensor(k12_t[:], k12_t[:], tmpb[:], Op.add),
                   deps=[tq1, tq2])
        tq4 = p.do("vector", lambda e: e.tensor_scalar_mul(k12_t[:], k12_t[:], TWO_PI),
                   deps=[tq3])
        tq5 = p.do("vector", lambda e: e.tensor_tensor(tmpb[:], rr_t[:], k12_t[:], Op.subtract),
                   deps=[tq4])
        tcos = p.do("scalar", lambda e: e.activation(cos_t[:], tmpb[:], AF.Sin,
                                                     bias=halfpi_col[:], scale=1.0),
                    deps=[tq5, t_hp, LU.get("cos_w")])
        LU["rr"] = tq5
        LU["k12"] = tq5
        LU["tmpa"] = tsin
        LU["tmpb"] = tcos
        LU["pos"] = tr
        return tcos, tsin

    def rope_apply(ps_ap, oh, dst_ap, dep_group, tcos, tsin):
        co = cos_t[:, 256 * oh:256 * oh + 256]
        si = sin_t[:, 256 * oh:256 * oh + 256]
        ev, od = ps_ap[:, 0:512:2], ps_ap[:, 1:512:2]
        dev, dod = dst_ap[:, 0:512:2], dst_ap[:, 1:512:2]
        t1 = p.do("vector", lambda e: e.tensor_tensor(tmpa[:, 0:256], ev, co, Op.mult),
                  deps=[dep_group, tcos, LU.get("tmpa")])
        t2 = p.do("vector", lambda e: e.tensor_tensor(tmpb[:, 0:256], od, si, Op.mult),
                  deps=[dep_group, tsin, LU.get("tmpb")])
        t3 = p.do("vector", lambda e: e.tensor_tensor(dev, tmpa[:, 0:256], tmpb[:, 0:256],
                                                      Op.subtract), deps=[t1, t2])
        t4 = p.do("vector", lambda e: e.tensor_tensor(tmpa[:, 0:256], od, co, Op.mult),
                  deps=[dep_group, t3])
        t5 = p.do("vector", lambda e: e.tensor_tensor(tmpb[:, 0:256], ev, si, Op.mult),
                  deps=[dep_group, t3])
        t6 = p.do("vector", lambda e: e.tensor_tensor(dod, tmpa[:, 0:256], tmpb[:, 0:256],
                                                      Op.add), deps=[t4, t5])
        LU["tmpa"] = t6
        LU["tmpb"] = t6
        return t6, [t1, t2, t4, t5]

    def x4(t):
        return t.rearrange("p (jj c2 k) -> p jj c2 k", c2=4, k=128)

    def ln_transpose_tile(r0_tile, spill):
        toks_all = []
        for c in range(4):
            buf = xt[c % 2]
            r0 = r0_tile + 128 * c
            tl = p.do("sync", lambda e, buf=buf, r0=r0: e.dma_start(
                out=buf[:], in_=xv[r0:r0 + 128, :]), deps=[LU.get(("xt", c % 2))],
                dma=f"xt{c % 2}")
            ts = ln_stats(tl, buf[:])
            txn = p.do("scalar", lambda e, buf=buf: e.activation(
                xn[:], buf[:], AF.Identity, bias=ln_m[:], scale=ln_c[3][:]),
                deps=[ts, LU.get("xn")])
            LU[("xt", c % 2)] = txn
            ev_last = None
            for half in range(2):
                bkid = half
                mms = []
                for jj in range(4):
                    j = 4 * half + jj
                    mms.append(((lambda j=j: xn[:, 128 * j:128 * j + 128]),
                                (lambda: ident[:]),
                                [txn, t_id], True, PB(bkid)[:, 128 * jj:128 * jj + 128]))
                g = mm_group(bkid, mms)
                th_ = p.do("vector", lambda e, bkid=bkid, half=half, c=c: e.tensor_copy(
                    x4(xnTh_t)[:, 4 * half:4 * half + 4, c, :],
                    PB(bkid)[:, 0:512].rearrange("p (j k) -> p j k", k=128)),
                    deps=[g, LU.get("xnT_t"), LU.get("xnT_sp_last")])
                bk.read(bkid, th_)
                tl_ = p.do("vector", lambda e, bkid=bkid, half=half, c=c:
                           e.scalar_tensor_tensor(
                    x4(xnTl_t)[:, 4 * half:4 * half + 4, c, :],
                    PB(bkid)[:, 0:512].rearrange("p (j k) -> p j k", k=128), 1.0,
                    x4(xnTh_t)[:, 4 * half:4 * half + 4, c, :],
                    Op.mult, Op.subtract), deps=[g, th_])
                bk.read(bkid, tl_)
                ev_last = tl_
                toks_all.append(th_)
                toks_all.append(tl_)
            LU["xn"] = ev_last
        if spill:
            for j in range(8):
                LU["xnT_sp_last"] = p.do("sync", lambda e, j=j, r0_tile=r0_tile: e.dma_start(
                    out=xnT_hbm[128 * j:128 * j + 128, 0, r0_tile:r0_tile + 512],
                    in_=xnTh_t[:, 512 * j:512 * j + 512]), deps=list(toks_all),
                    dma="xnT_sp")
                LU["xnT_sp_last"] = p.do("sync", lambda e, j=j, r0_tile=r0_tile: e.dma_start(
                    out=xnT_hbm[128 * j:128 * j + 128, 1, r0_tile:r0_tile + 512],
                    in_=xnTl_t[:, 512 * j:512 * j + 512]), deps=list(toks_all),
                    dma="xnT_sp")
        return toks_all

    wa1 = contextlib.ExitStack()
    load_brow(wa1, "bkr", "biasA1")
    load_brow(wa1, "buv", "biasA1")
    t_b["bkr"] = t_b["buv"]
    tw_dkv = load_w_hl(wa1, "wdkv", D, DKV)
    tw_kr = load_w_hl(wa1, "wkr", D, D)
    tw_uk = load_w_hl(wa1, "wuk", DKV, D)
    tw_uv = load_w_hl(wa1, "wuv", DKV, D)

    vones = []
    t_vl0 = p.do("gpsimd", lambda e: e.memset(vl_t[:], 0.0))
    for c in range(2):
        for hg in range(16):
            vones.append(p.do("gpsimd", lambda e, c=c, hg=hg: e.memset(
                vh_t[:, 1040 * c + 65 * hg + 64: 1040 * c + 65 * hg + 65], 1.0)))
    t_vones = vones[-1]

    tkm = None
    for st in range(8):
        r0_tile = 512 * st
        xnT_toks = ln_transpose_tile(r0_tile, spill=True)
        ckv_evac = []
        for oc in range(4):
            bkid = 2 + (oc % 2)
            mms = []
            for ic in range(8):
                mms += p3((lambda ic=ic, oc=oc: WH("wdkv", ic, 128 * oc, 128 * oc + 128)),
                          (lambda ic=ic, oc=oc: WL("wdkv", ic, 128 * oc, 128 * oc + 128)),
                          (lambda ic=ic: xnTh_t[:, 512 * ic:512 * ic + 512]),
                          (lambda ic=ic: xnTl_t[:, 512 * ic:512 * ic + 512]),
                          [tw_dkv[ic], xnT_toks[-1]])
            g = mm_group(bkid, mms)
            tvh = p.do("scalar", lambda e, oc=oc, bkid=bkid: e.activation(
                ckvh_t[:, 512 * oc:512 * oc + 512], PB(bkid)[:, 0:512], AF.Identity,
                bias=bcol["bdkv"][:, oc:oc + 1], scale=1.0),
                deps=[g, t_b["bdkv"], LU.get("ckvT")])
            bk.read(bkid, tvh)
            tvl_ = p.do("vector", lambda e, oc=oc, bkid=bkid: e.scalar_tensor_tensor(
                ckvl_t[:, 512 * oc:512 * oc + 512], PB(bkid)[:, 0:512],
                bcol["bdkv"][:, oc:oc + 1], ckvh_t[:, 512 * oc:512 * oc + 512],
                Op.add, Op.subtract), deps=[g, tvh])
            bk.read(bkid, tvl_)
            ckv_evac.append(tvl_)
        LU["ckvT"] = ckv_evac[-1]

        rope_done = []
        for c in range(4):
            tcos, tsin = cos_sin(r0_tile + 128 * c)
            for oh in range(2):
                bkid = 4 + oh
                mms = []
                for ic in range(8):
                    mms += p3(
                        (lambda ic=ic, c=c: xnTh_t[:, 512 * ic + 128 * c:
                                                   512 * ic + 128 * c + 128]),
                        (lambda ic=ic, c=c: xnTl_t[:, 512 * ic + 128 * c:
                                                   512 * ic + 128 * c + 128]),
                        (lambda ic=ic, oh=oh: WH("wkr", ic, 512 * oh, 512 * oh + 512)),
                        (lambda ic=ic, oh=oh: WL("wkr", ic, 512 * oh, 512 * oh + 512)),
                        [tw_kr[ic], xnT_toks[-1]])
                mms.append(((lambda: ones8[0:1, :]),
                            (lambda oh=oh: brow["bkr"][:, 512 * oh:512 * oh + 512]),
                            [t_ones, t_b["bkr"]], False, None))
                g = mm_group(bkid, mms, extra_deps=[LU.get(("krp", c, oh))])
                LU["xnT_t"] = g
                dst = big_t[:, 1024 * c + 512 * oh: 1024 * c + 512 * oh + 512]
                t6, readers = rope_apply(PB(bkid)[:, 0:512], oh, dst, g, tcos, tsin)
                for rt in readers + [t6]:
                    bk.read(bkid, rt)
                LU[("krp", c, oh)] = t6
                rope_done.append(t6)

        kt_evh = []
        for oc in range(8):
            bkid = 6 + (oc % 2)
            mms = []
            for cc in range(4):
                mms += p3((lambda cc=cc, oc=oc: WH("wuk", cc, 128 * oc, 128 * oc + 128)),
                          (lambda cc=cc, oc=oc: WL("wuk", cc, 128 * oc, 128 * oc + 128)),
                          (lambda cc=cc: ckvh_t[:, 512 * cc:512 * cc + 512]),
                          (lambda cc=cc: ckvl_t[:, 512 * cc:512 * cc + 512]),
                          [tw_uk[cc], ckv_evac[-1]])
            for c in range(4):
                mms.append(((lambda c=c, oc=oc: big_t[:, 1024 * c + 128 * oc:
                                                      1024 * c + 128 * oc + 128]),
                            (lambda: ident[:]),
                            [rope_done[-1], t_id], True, PB(bkid)[:, 128 * c:128 * c + 128]))
            g = mm_group(bkid, mms)
            th_ = p.do("scalar", lambda e, oc=oc, bkid=bkid: e.activation(
                outh_t[:, 512 * oc:512 * oc + 512], PB(bkid)[:, 0:512], AF.Identity,
                bias=bcol["buk"][:, oc:oc + 1], scale=1.0),
                deps=[g, t_b["buk"], LU.get(("outT", oc))])
            kt_evh.append(th_)
            bk.read(bkid, th_)
            tl_ = p.do("vector", lambda e, oc=oc, bkid=bkid: e.scalar_tensor_tensor(
                outl_t[:, 512 * oc:512 * oc + 512], PB(bkid)[:, 0:512],
                bcol["buk"][:, oc:oc + 1], outh_t[:, 512 * oc:512 * oc + 512],
                Op.add, Op.subtract), deps=[g, th_])
            bk.read(bkid, tl_)
            t1 = p.do("sync", lambda e, oc=oc, st=st: e.dma_start(
                out=kT_hbm[128 * oc:128 * oc + 128, 4 * st:4 * st + 4, 0, :],
                in_=outh_t[:, 512 * oc:512 * oc + 512]
                    .rearrange("p (kc k) -> p kc k", k=128)), deps=[th_, tl_],
                dma=f"osp{oc}")
            tsp = p.do("sync", lambda e, oc=oc, st=st: e.dma_start(
                out=kT_hbm[128 * oc:128 * oc + 128, 4 * st:4 * st + 4, 1, :],
                in_=outl_t[:, 512 * oc:512 * oc + 512]
                    .rearrange("p (kc k) -> p kc k", k=128)), deps=[th_, tl_, t1],
                dma=f"osp{oc}")
            LU[("outT", oc)] = tsp

        for c in range(4):
            cs = c % 2
            ev_t = []
            for oh in range(2):
                bkid = 4 + oh
                mms = []
                for cc in range(4):
                    mms += p3(
                        (lambda cc=cc, c=c: ckvh_t[:, 512 * cc + 128 * c:
                                                   512 * cc + 128 * c + 128]),
                        (lambda cc=cc, c=c: ckvl_t[:, 512 * cc + 128 * c:
                                                   512 * cc + 128 * c + 128]),
                        (lambda cc=cc, oh=oh: WH("wuv", cc, 512 * oh, 512 * oh + 512)),
                        (lambda cc=cc, oh=oh: WL("wuv", cc, 512 * oh, 512 * oh + 512)),
                        [tw_uv[cc], ckv_evac[-1]])
                mms.append(((lambda: ones8[0:1, :]),
                            (lambda oh=oh: brow["buv"][:, 512 * oh:512 * oh + 512]),
                            [t_ones, t_b["buv"]], False, None))
                g = mm_group(bkid, mms, extra_deps=[LU.get(("vev", cs, oh))])
                vh_dst = vh_t[:, 1040 * cs + 520 * oh: 1040 * cs + 520 * oh + 520] \
                    .rearrange("p (h j) -> p h j", j=65)[:, :, 0:64]
                vl_dst = vl_t[:, 1040 * cs + 520 * oh: 1040 * cs + 520 * oh + 520] \
                    .rearrange("p (h j) -> p h j", j=65)[:, :, 0:64]
                th_ = p.do("vector", lambda e, vh_dst=vh_dst, bkid=bkid: e.tensor_copy(
                    vh_dst, PB(bkid)[:, 0:512].rearrange("p (h j) -> p h j", j=64)),
                    deps=[g, t_vones, LU.get(("vsp", cs))])
                bk.read(bkid, th_)
                tl_ = p.do("vector", lambda e, vh_dst=vh_dst, vl_dst=vl_dst, bkid=bkid:
                           e.scalar_tensor_tensor(
                    vl_dst, PB(bkid)[:, 0:512].rearrange("p (h j) -> p h j", j=64), 1.0,
                    vh_dst, Op.mult, Op.subtract),
                    deps=[g, th_, t_vl0, LU.get(("vsp", cs))])
                bk.read(bkid, tl_)
                ev_t += [th_, tl_]
                LU[("vev", cs, oh)] = tl_
            tsp = p.do("sync", lambda e, cs=cs, c=c, r0_tile=r0_tile: e.dma_start(
                out=v_hbm[r0_tile + 128 * c: r0_tile + 128 * c + 128, 0, :],
                in_=vh_t[:, 1040 * cs:1040 * cs + 1040]), deps=ev_t, dma=f"vsp{cs}")
            tsp = p.do("sync", lambda e, cs=cs, c=c, r0_tile=r0_tile: e.dma_start(
                out=v_hbm[r0_tile + 128 * c: r0_tile + 128 * c + 128, 1, :],
                in_=vl_t[:, 1040 * cs:1040 * cs + 1040]), deps=ev_t + [tsp], dma=f"vsp{cs}")
            LU[("vsp", cs)] = tsp

        tsq = p.do("scalar", lambda e: e.activation(big_t[:], outh_t[:], AF.Square,
                                                    bias=0.0, scale=1.0),
                   deps=kt_evh + [rope_done[-1], LU.get("big")])
        mms = []
        for ic in range(8):
            mms.append(((lambda ic=ic: e_sb[:, 16 * ic:16 * ic + 16]),
                        (lambda ic=ic: big_t[:, 512 * ic:512 * ic + 512]),
                        [tsq] + ([t_es[-1]] if ic == 0 else []), False, PB(2)[0:16, 0:512]))
        g = mm_group(2, mms)
        trm = p.do("vector", lambda e: e.reduce_max(t16[:], PB(2)[0:16, 0:512], AX.X),
                   deps=[g, LU.get("t16")])
        bk.read(2, trm)
        tkm = p.do("vector", lambda e: e.tensor_max(kmax_c[:], kmax_c[:], t16[:]),
                   deps=[trm, t_kin])
        LU["t16"] = tkm
        LU["big"] = trm

    p.barrier()
    wa1.close()
    if cut == "A1":
        return _finish()

    wa2 = contextlib.ExitStack()
    load_brow(wa2, "bqr", "biasA2")
    tw_dq = load_w_hl(wa2, "wdq", D, DQ)
    tw_uq = load_w_hl(wa2, "wuq", DQ, D)
    tw_qr = load_w_hl(wa2, "wqr", DQ, D)

    tqm = None
    for stq in range(4):
        r0 = 512 * stq
        xnT_toks = []
        for j in range(8):
            xnT_toks.append(p.do("sync", lambda e, j=j, r0=r0: e.dma_start(
                out=xnTh_t[:, 512 * j:512 * j + 512],
                in_=xnT_hbm[128 * j:128 * j + 128, 0, r0:r0 + 512]),
                deps=[LU.get("xnT_t")], dma=f"xnT_rl{j % 4}"))
            xnT_toks.append(p.do("sync", lambda e, j=j, r0=r0: e.dma_start(
                out=xnTl_t[:, 512 * j:512 * j + 512],
                in_=xnT_hbm[128 * j:128 * j + 128, 1, r0:r0 + 512]),
                deps=[LU.get("xnT_t")], dma=f"xnT_rl{j % 4}"))
        xnT_last = xnT_toks[9::2]

        cq_evac = []
        for oc in range(4):
            bkid = 2 + (oc % 2)
            mms = []
            for ic in range(8):
                mms += p3((lambda ic=ic, oc=oc: WH("wdq", ic, 128 * oc, 128 * oc + 128)),
                          (lambda ic=ic, oc=oc: WL("wdq", ic, 128 * oc, 128 * oc + 128)),
                          (lambda ic=ic: xnTh_t[:, 512 * ic:512 * ic + 512]),
                          (lambda ic=ic: xnTl_t[:, 512 * ic:512 * ic + 512]),
                          [tw_dq[ic]] + xnT_last)
            g = mm_group(bkid, mms)
            LU["xnT_t"] = g
            tvh = p.do("scalar", lambda e, oc=oc, bkid=bkid: e.activation(
                ckvh_t[:, 512 * oc:512 * oc + 512], PB(bkid)[:, 0:512], AF.Identity,
                bias=bcol["bdq"][:, oc:oc + 1], scale=1.0),
                deps=[g, t_b["bdq"], LU.get("ckvT")])
            bk.read(bkid, tvh)
            tvl_ = p.do("vector", lambda e, oc=oc, bkid=bkid: e.scalar_tensor_tensor(
                ckvl_t[:, 512 * oc:512 * oc + 512], PB(bkid)[:, 0:512],
                bcol["bdq"][:, oc:oc + 1], ckvh_t[:, 512 * oc:512 * oc + 512],
                Op.add, Op.subtract), deps=[g, tvh])
            bk.read(bkid, tvl_)
            cq_evac.append(tvl_)
        LU["ckvT"] = cq_evac[-1]
        if cut == "A2q" and stq == 0:
            return _finish()

        rope_done = []
        for c in range(4):
            tcos, tsin = cos_sin(r0 + 128 * c)
            for oh in range(2):
                bkid = 4 + oh
                mms = []
                for cc in range(4):
                    mms += p3(
                        (lambda cc=cc, c=c: ckvh_t[:, 512 * cc + 128 * c:
                                                   512 * cc + 128 * c + 128]),
                        (lambda cc=cc, c=c: ckvl_t[:, 512 * cc + 128 * c:
                                                   512 * cc + 128 * c + 128]),
                        (lambda cc=cc, oh=oh: WH("wqr", cc, 512 * oh, 512 * oh + 512)),
                        (lambda cc=cc, oh=oh: WL("wqr", cc, 512 * oh, 512 * oh + 512)),
                        [tw_qr[cc], cq_evac[-1]])
                mms.append(((lambda: ones8[0:1, :]),
                            (lambda oh=oh: brow["bqr"][:, 512 * oh:512 * oh + 512]),
                            [t_ones, t_b["bqr"]], False, None))
                g = mm_group(bkid, mms, extra_deps=[LU.get(("krp", c, oh))])
                dst = big_t[:, 1024 * c + 512 * oh: 1024 * c + 512 * oh + 512]
                t6, readers = rope_apply(PB(bkid)[:, 0:512], oh, dst, g, tcos, tsin)
                for rt in readers + [t6]:
                    bk.read(bkid, rt)
                LU[("krp", c, oh)] = t6
                rope_done.append(t6)
        if cut == "A2r" and stq == 0:
            return _finish()

        qt_evh = []
        for oc in range(8):
            bkid = 6 + (oc % 2)
            mms = []
            for cc in range(4):
                mms += p3((lambda cc=cc, oc=oc: WH("wuq", cc, 128 * oc, 128 * oc + 128)),
                          (lambda cc=cc, oc=oc: WL("wuq", cc, 128 * oc, 128 * oc + 128)),
                          (lambda cc=cc: ckvh_t[:, 512 * cc:512 * cc + 512]),
                          (lambda cc=cc: ckvl_t[:, 512 * cc:512 * cc + 512]),
                          [tw_uq[cc], cq_evac[-1]])
            for c in range(4):
                mms.append(((lambda c=c, oc=oc: big_t[:, 1024 * c + 128 * oc:
                                                      1024 * c + 128 * oc + 128]),
                            (lambda: ident[:]),
                            [rope_done[-1], t_id], True, PB(bkid)[:, 128 * c:128 * c + 128]))
            g = mm_group(bkid, mms)
            th_ = p.do("scalar", lambda e, oc=oc, bkid=bkid: e.activation(
                outh_t[:, 512 * oc:512 * oc + 512], PB(bkid)[:, 0:512], AF.Identity,
                bias=buqs[:, oc:oc + 1], scale=INV_SCALE),
                deps=[g, t_buqs, LU.get(("outT", oc))])
            qt_evh.append(th_)
            bk.read(bkid, th_)
            tfull = p.do("scalar", lambda e, oc=oc, bkid=bkid: e.activation(
                tmpa[:], PB(bkid)[:, 0:512], AF.Identity,
                bias=buqs[:, oc:oc + 1], scale=INV_SCALE),
                deps=[g, t_buqs, LU.get("tmpa")])
            bk.read(bkid, tfull)
            tl_ = p.do("vector", lambda e, oc=oc: e.tensor_tensor(
                outl_t[:, 512 * oc:512 * oc + 512], tmpa[:],
                outh_t[:, 512 * oc:512 * oc + 512], Op.subtract), deps=[tfull, th_])
            LU["tmpa"] = tl_
            t1 = p.do("sync", lambda e, oc=oc, r0=r0: e.dma_start(
                out=qT_hbm[128 * oc:128 * oc + 128, 0, r0:r0 + 512],
                in_=outh_t[:, 512 * oc:512 * oc + 512]), deps=[th_, tl_], dma=f"osp{oc}")
            tsp = p.do("sync", lambda e, oc=oc, r0=r0: e.dma_start(
                out=qT_hbm[128 * oc:128 * oc + 128, 1, r0:r0 + 512],
                in_=outl_t[:, 512 * oc:512 * oc + 512]), deps=[th_, tl_, t1], dma=f"osp{oc}")
            LU[("outT", oc)] = tsp
        if cut == "A2s" and stq == 0:
            return _finish()

        tsq = p.do("scalar", lambda e: e.activation(big_t[:], outh_t[:], AF.Square,
                                                    bias=0.0, scale=1.0),
                   deps=qt_evh + [rope_done[-1], LU.get("big")])
        mms = []
        for ic in range(8):
            mms.append(((lambda ic=ic: e_sb[:, 16 * ic:16 * ic + 16]),
                        (lambda ic=ic: big_t[:, 512 * ic:512 * ic + 512]),
                        [tsq], False, PB(2)[0:16, 0:512]))
        g = mm_group(2, mms)
        trm = p.do("vector", lambda e: e.reduce_max(t16[:], PB(2)[0:16, 0:512], AX.X),
                   deps=[g, LU.get("t16")])
        bk.read(2, trm)
        tqm = p.do("vector", lambda e: e.tensor_max(qmax_c[:], qmax_c[:], t16[:]),
                   deps=[trm, t_qin])
        LU["t16"] = tqm
        LU["big"] = trm
        if cut == "A2x" and stq == 0:
            return _finish()

    g1 = mm_group(3, [((lambda: kmax_c[:]), (lambda: ident[0:16, 0:16]),
                       [tkm, tqm, t_id], True, PB(3)[0:1, 0:16])])
    tkr = p.do("vector", lambda e: e.tensor_copy(mrow[:, 0:16], PB(3)[0:1, 0:16]), deps=[g1])
    bk.read(3, tkr)
    g2 = mm_group(3, [((lambda: qmax_c[:]), (lambda: ident[0:16, 0:16]),
                       [tqm, t_id], True, PB(3)[0:1, 0:16])])
    tqr2 = p.do("vector", lambda e: e.tensor_copy(mrow[:, 16:32], PB(3)[0:1, 0:16]), deps=[g2])
    bk.read(3, tqr2)
    tpr = p.do("vector", lambda e: e.tensor_tensor(
        mrow[:, 0:16], mrow[:, 0:16], mrow[:, 16:32], Op.mult), deps=[tkr, tqr2])
    tm2 = p.do("vector", lambda e: e.reduce_max(mrow[:, 32:33], mrow[:, 0:16], AX.X), deps=[tpr])
    tm1 = p.do("scalar", lambda e: e.activation(mrow[:, 32:33], mrow[:, 32:33], AF.Sqrt,
                                                bias=0.0, scale=1.0), deps=[tm2])
    tmn = p.do("vector", lambda e: e.tensor_scalar_mul(mrow[:, 32:33], mrow[:, 32:33], -1.0),
               deps=[tm1])
    g3 = mm_group(3, [((lambda: ones8[0:1, :]), (lambda: mrow[:, 32:33]),
                       [tmn, t_ones], False, PB(3)[:, 0:1])])
    t_negm = p.do("vector", lambda e: e.tensor_copy(negM[:], PB(3)[:, 0:1]), deps=[g3])
    bk.read(3, t_negm)
    if d_negm is not None:
        p.do("sync", lambda e: e.dma_start(out=d_negm[:, :], in_=negM[0:1, :]),
             deps=[t_negm], dma="dump")

    p.barrier()
    wa2.close()
    astack.close()
    if cut == "A2":
        return _finish()

    cdstack = contextlib.ExitStack()
    xn2Th_sb = mk_sb(cdstack)("xn2Th_sb", [128, 8 * 2048], BF16)
    xn2Tl_sb = mk_sb(cdstack)("xn2Tl_sb", [128, 8 * 2048], BF16)
    bcstack = contextlib.ExitStack()
    OTh_sb = mk_sb(bcstack)("OTh_sb", [128, 8 * 2048], BF16)
    OTl_sb = mk_sb(bcstack)("OTl_sb", [128, 8 * 2048], BF16)
    bstack = contextlib.ExitStack()
    bsb = mk_sb(bstack)
    qt_sb = bsb("qt_sb", [128, 4 * 2048], BF16)
    kt_sb = [bsb(f"kt_sb{i}", [128, 512], BF16) for i in range(2)]
    vt_sb = [bsb(f"vt_sb{i}", [128, 2 * 260], BF16) for i in range(2)]
    pt32 = bsb("pt32", [128, 4 * 512])
    pth = bsb("pth", [128, 4 * 512], BF16)
    ptl = bsb("ptl", [128, 4 * 512], BF16)
    o32 = [sq_scr[0:64, 512:1024]]
    lrec_row = sq_scr[0:1, 0:512]
    oh_t = [bsb("oh_t0", [64, 512], BF16)]
    ol_t = [bsb("ol_t0", [64, 512], BF16)]

    tev = None
    for Q in range(4):
        qload = []
        for hp in range(2):
            for hl in range(2):
                qload.append(p.do("sync", lambda e, Q=Q, hp=hp, hl=hl: e.dma_start(
                    out=qt_sb[:, 2048 * (2 * hp + hl):2048 * (2 * hp + hl) + 2048],
                    in_=qT_hbm[256 * Q + 128 * hp:256 * Q + 128 * hp + 128, hl, :]),
                    deps=[LU.get("qt_sb")], dma=f"qld{2 * hp + hl}"))
        for qt in range(4):
            av_tok = [None] * 4

            def emit_av(pend):
                kcp, bufip, tvlp, infos = pend
                last = None
                for hh, thv, tlv in infos:
                    vh = lambda bufip=bufip, hh=hh: vt_sb[bufip][:, 65 * hh:65 * hh + 65]
                    vl = lambda bufip=bufip, hh=hh: vt_sb[bufip][:, 260 + 65 * hh:
                                                                  260 + 65 * hh + 65]
                    ph = lambda hh=hh: pth[:, 512 * hh:512 * hh + 512]
                    pl_ = lambda hh=hh: ptl[:, 512 * hh:512 * hh + 512]
                    tav = None
                    for pi, (vf, pf, dtok) in enumerate(
                            [(vh, ph, thv), (vh, pl_, tlv), (vl, ph, thv)]):
                        dd = [dtok, tvlp]
                        if kcp == 0 and pi == 0:
                            dd = dd + bk.pre_write(hh)
                        tav = p.do("tensor", lambda e, vf=vf, pf=pf, hh=hh, kcp=kcp, pi=pi:
                                   e.matmul(
                            PB(hh)[0:65, 0:512], vf(), pf(),
                            start=(kcp == 0 and pi == 0),
                            stop=(kcp == 31 and pi == 2)), deps=dd)
                    if kcp == 0:
                        bk.wrote(hh, tav)
                    av_tok[hh] = tav
                    LU[("pt", hh)] = tav
                    last = tav
                LU[("vt", bufip)] = last

            pend = None
            for kc in range(32):
                bufi = kc % 2
                tk = p.do("sync", lambda e, Q=Q, kc=kc, bufi=bufi: e.dma_start(
                    out=kt_sb[bufi][:].rearrange("p (a hl k) -> p a hl k", a=2, k=128),
                    in_=kT_hbm[256 * Q:256 * Q + 256, kc, :, :]
                        .rearrange("(a p) hl k -> p a hl k", p=128)),
                    deps=[LU.get(("kt", bufi))], dma=f"kt{bufi}")
                tvl = p.do("sync", lambda e, Q=Q, kc=kc, bufi=bufi: e.dma_start(
                    out=vt_sb[bufi][:].rearrange("p (hl c) -> p hl c", hl=2),
                    in_=v_hbm[128 * kc:128 * kc + 128, :, 260 * Q:260 * Q + 260]),
                    deps=[LU.get(("vt", bufi))], dma=f"vt{bufi}")
                sc_tok = [None] * 4
                for hp in range(2):
                    sb0 = 4 + 2 * hp
                    sb1 = 5 + 2 * hp
                    kh = lambda bufi=bufi, hp=hp: kt_sb[bufi][:, 256 * hp:256 * hp + 128]
                    kl = lambda bufi=bufi, hp=hp: kt_sb[bufi][:, 256 * hp + 128:
                                                              256 * hp + 256]
                    qh = lambda hp=hp, qt=qt: qt_sb[:, 4096 * hp + 512 * qt:
                                                    4096 * hp + 512 * qt + 512]
                    ql = lambda hp=hp, qt=qt: qt_sb[:, 4096 * hp + 2048 + 512 * qt:
                                                    4096 * hp + 2048 + 512 * qt + 512]
                    deps_s = [tk] + qload
                    tA = tB = None
                    for pi, (kf, qf) in enumerate([(kh, qh), (kh, ql), (kl, qh)]):
                        tA = p.do("tensor", lambda e, kf=kf, qf=qf, sb0=sb0, pi=pi: e.matmul(
                            PB(sb0)[:, 0:512], kf()[0:64, :], qf()[0:64, :],
                            start=(pi == 0), stop=(pi == 2)),
                            deps=deps_s + (bk.pre_write(sb0) if pi == 0 else []))
                    for pi, (kf, qf) in enumerate([(kh, qh), (kh, ql), (kl, qh)]):
                        tB = p.do("tensor", lambda e, kf=kf, qf=qf, sb1=sb1, pi=pi: e.matmul(
                            PB(sb1)[:, 0:512], kf()[64:128, :], qf()[64:128, :],
                            start=(pi == 0), stop=(pi == 2), tile_position=(64, 0)),
                            deps=deps_s + (bk.pre_write(sb1) if pi == 0 else []))
                    bk.wrote(sb0, tA)
                    bk.wrote(sb1, tB)
                    sc_tok[2 * hp] = tA
                    sc_tok[2 * hp + 1] = tB
                LU[("kt", bufi)] = sc_tok[3]
                if pend is not None:
                    emit_av(pend)
                infos = []
                for hh in range(4):
                    sbb = 4 + hh
                    tex = p.do("scalar", lambda e, sbb=sbb, hh=hh: e.activation(
                        pt32[:, 512 * hh:512 * hh + 512], PB(sbb)[:, 0:512],
                        AF.Exp, bias=negM[:], scale=1.0),
                        deps=[sc_tok[hh], t_negm, LU.get(("pt", hh))])
                    bk.read(sbb, tex)
                    thv = p.do("vector", lambda e, hh=hh: e.tensor_copy(
                        pth[:, 512 * hh:512 * hh + 512],
                        pt32[:, 512 * hh:512 * hh + 512]), deps=[tex])
                    tlv = p.do("vector", lambda e, hh=hh: e.tensor_tensor(
                        ptl[:, 512 * hh:512 * hh + 512],
                        pt32[:, 512 * hh:512 * hh + 512],
                        pth[:, 512 * hh:512 * hh + 512], Op.subtract),
                        deps=[thv])
                    infos.append((hh, thv, tlv))
                pend = (kc, bufi, tvl, infos)
            emit_av(pend)
            for hh in range(4):
                g = 4 * Q + hh
                fc = g // 2
                prow = 64 * (g % 2)
                trc = p.do("vector", lambda e, hh=hh: e.reciprocal(
                    lrec_row, PB(hh)[64:65, 0:512]),
                    deps=[av_tok[hh], LU.get("lrec_row")])
                bk.read(hh, trc)
                gbc = mm_group(7, [((lambda: ones8[0:1, 0:64]), (lambda: lrec_row),
                                    [trc, t_ones], False, PB(7)[0:64, 0:512])])
                obuf = o32[0]
                tcp = p.do("vector", lambda e, hh=hh, obuf=obuf: e.tensor_copy(
                    obuf, PB(hh)[0:64, 0:512]),
                    deps=[av_tok[hh], LU.get("o32")])
                bk.read(hh, tcp)
                tnm = p.do("vector", lambda e, obuf=obuf: e.tensor_tensor(
                    obuf, obuf, PB(7)[0:64, 0:512], Op.mult), deps=[tcp, gbc])
                bk.read(7, tnm)
                LU["lrec_row"] = tnm
                ohb, olb = oh_t[0], ol_t[0]
                tch = p.do("vector", lambda e, obuf=obuf, ohb=ohb: e.tensor_copy(
                    ohb[:], obuf), deps=[tnm, LU.get("ohl")])
                tcl = p.do("vector", lambda e, obuf=obuf, ohb=ohb, olb=olb:
                           e.scalar_tensor_tensor(
                    olb[:], obuf, 1.0, ohb[:], Op.mult, Op.subtract), deps=[tch])
                LU["o32"] = tcl
                t1 = p.do("gpsimd", lambda e, fc=fc, prow=prow, qt=qt, ohb=ohb: e.dma_start(
                    out=OTh_sb[prow:prow + 64,
                               2048 * fc + 512 * qt: 2048 * fc + 512 * qt + 512],
                    in_=ohb[:]), deps=[tcl], dma="oev0")
                tev = p.do("gpsimd", lambda e, fc=fc, prow=prow, qt=qt, olb=olb: e.dma_start(
                    out=OTl_sb[prow:prow + 64,
                               2048 * fc + 512 * qt: 2048 * fc + 512 * qt + 512],
                    in_=olb[:]), deps=[tcl, t1], dma="oev1")
                LU["ohl"] = tev
        LU["qt_sb"] = tev

    if d_OT is not None:
        for j in range(8):
            p.do("gpsimd", lambda e, j=j: e.dma_start(
                out=d_OT[128 * j:128 * j + 128, :],
                in_=OTh_sb[:, 2048 * j:2048 * j + 2048]), deps=[tev], dma="dumpg")

    p.barrier()
    bstack.close()
    if cut == "B":
        return _finish()

    wcstack = contextlib.ExitStack()
    load_brow(wcstack, "bo", "biasC")
    tw_o = load_w_hl(wcstack, "wo", D, D)
    ao_b = mk_sb(wcstack)("ao_b", [128, 1024])

    def x16(t):
        return t.rearrange("p (jj c2 k) -> p jj c2 k", c2=16, k=128)

    for qc in range(16):
        aa = ao_t if qc % 2 == 0 else ao_b
        ao_toks = []
        for oh in range(2):
            bkid = oh
            mms = []
            for dc in range(8):
                mms += p3((lambda dc=dc, qc=qc: OTh_sb[:, 2048 * dc + 128 * qc:
                                                       2048 * dc + 128 * qc + 128]),
                          (lambda dc=dc, qc=qc: OTl_sb[:, 2048 * dc + 128 * qc:
                                                       2048 * dc + 128 * qc + 128]),
                          (lambda dc=dc, oh=oh: WH("wo", dc, 512 * oh, 512 * oh + 512)),
                          (lambda dc=dc, oh=oh: WL("wo", dc, 512 * oh, 512 * oh + 512)),
                          [tw_o[-1]])
            mms.append(((lambda: ones8[0:1, :]),
                        (lambda oh=oh: brow["bo"][:, 512 * oh:512 * oh + 512]),
                        [t_ones, t_b["bo"]], False, None))
            g = mm_group(bkid, mms)
            tv = p.do("scalar", lambda e, oh=oh, bkid=bkid, aa=aa: e.activation(
                aa[:, 512 * oh:512 * oh + 512], PB(bkid)[:, 0:512], AF.Identity,
                bias=0.0, scale=1.0), deps=[g, LU.get(("aoC", qc % 2))])
            bk.read(bkid, tv)
            ao_toks.append(tv)
        ts = ln_stats(ao_toks[-1], aa[:], dep_extra=[ao_toks[0]])
        tln = p.do("scalar", lambda e, aa=aa: e.activation(
            lnout[:], aa[:], AF.Identity, bias=ln_m[:], scale=ln_c[3][:]),
            deps=[ts, ao_toks[0], ao_toks[1], LU.get("lnout")])
        LU[("aoC", qc % 2)] = tln
        tl = p.do("sync", lambda e, qc=qc: e.dma_start(
            out=xt[0][:], in_=xv[128 * qc:128 * qc + 128, :]),
            deps=[LU.get(("xt", 0))], dma="xt0")
        tres = p.do("vector", lambda e: e.tensor_tensor(res_t[:], xt[0][:], lnout[:], Op.add),
                    deps=[tl, tln, LU.get("res_t"), LU.get("res_dump"),
                          LU.get("res_sp_last")])
        LU[("xt", 0)] = tres
        LU["lnout"] = tres
        LU["res_sp_last"] = p.do("sync", lambda e, qc=qc: e.dma_start(
            out=res_hbm[128 * qc:128 * qc + 128, :], in_=res_t[:]), deps=[tres], dma="res_sp")
        if d_res is not None:
            LU["res_dump"] = p.do("sync", lambda e, qc=qc: e.dma_start(
                out=d_res[128 * qc:128 * qc + 128, :], in_=res_t[:]), deps=[tres], dma="dpr")
        ts2 = ln_stats(tres, res_t[:])
        txn2 = p.do("scalar", lambda e: e.activation(
            xn[:], res_t[:], AF.Identity, bias=ln_m[:], scale=ln_c[3][:]),
            deps=[ts2, LU.get("xn")])
        LU["res_t"] = txn2
        ev_last = None
        for half in range(2):
            bkid = 2 + half
            mms = []
            for jj in range(4):
                j = 4 * half + jj
                mms.append(((lambda j=j: xn[:, 128 * j:128 * j + 128]),
                            (lambda: ident[:]),
                            [txn2, t_id], True, PB(bkid)[:, 128 * jj:128 * jj + 128]))
            g = mm_group(bkid, mms)
            th_ = p.do("vector", lambda e, bkid=bkid, half=half, qc=qc: e.tensor_copy(
                x16(xn2Th_sb)[:, 4 * half:4 * half + 4, qc, :],
                PB(bkid)[:, 0:512].rearrange("p (j k) -> p j k", k=128)),
                deps=[g])
            bk.read(bkid, th_)
            tl_ = p.do("vector", lambda e, bkid=bkid, half=half, qc=qc:
                       e.scalar_tensor_tensor(
                x16(xn2Tl_sb)[:, 4 * half:4 * half + 4, qc, :],
                PB(bkid)[:, 0:512].rearrange("p (j k) -> p j k", k=128), 1.0,
                x16(xn2Th_sb)[:, 4 * half:4 * half + 4, qc, :],
                Op.mult, Op.subtract), deps=[g, th_])
            bk.read(bkid, tl_)
            ev_last = tl_
        LU["xn"] = ev_last

    p.barrier()
    wcstack.close()
    bcstack.close()
    if cut == "C":
        return _finish()

    dstack = contextlib.ExitStack()
    dsb = mk_sb(dstack)
    y_acc = dsb("y_acc", [128, 16 * 1024])
    rstack = contextlib.ExitStack()
    rsb = mk_sb(rstack)
    wr_raw = rsb("wr_raw", [8, 1024])
    wrTh_sb = rsb("wrTh_sb", [128, 8 * 8], BF16)
    wrTl_sb = rsb("wrTl_sb", [128, 8 * 8], BF16)
    ez_sb = rsb("ez_sb", [128, 8])
    zcol = rsb("zcol", [128, 1])

    twr = p.do("gpsimd", lambda e: e.dma_start(out=wr_raw[:], in_=win["wr"][:, :]), dma="wr")
    t_wrT = None
    for ic in range(8):
        g = mm_group(4, [((lambda ic=ic: wr_raw[0:8, 128 * ic:128 * ic + 128]),
                          (lambda: ident[0:8, 0:8]),
                          [twr, t_id], True, PB(4)[:, 0:8])])
        th_ = p.do("vector", lambda e, ic=ic: e.tensor_copy(
            wrTh_sb[:, 8 * ic:8 * ic + 8], PB(4)[:, 0:8]), deps=[g])
        bk.read(4, th_)
        t_wrT = p.do("vector", lambda e, ic=ic: e.scalar_tensor_tensor(
            wrTl_sb[:, 8 * ic:8 * ic + 8], PB(4)[:, 0:8], 1.0,
            wrTh_sb[:, 8 * ic:8 * ic + 8], Op.mult, Op.subtract), deps=[g, th_])
        bk.read(4, t_wrT)

    for qc in range(16):
        mms = []
        for ic in range(8):
            for lf, rf in [
                    ((lambda ic=ic, qc=qc: xn2Th_sb[:, 2048 * ic + 128 * qc:
                                                    2048 * ic + 128 * qc + 128]),
                     (lambda ic=ic: wrTh_sb[:, 8 * ic:8 * ic + 8])),
                    ((lambda ic=ic, qc=qc: xn2Th_sb[:, 2048 * ic + 128 * qc:
                                                    2048 * ic + 128 * qc + 128]),
                     (lambda ic=ic: wrTl_sb[:, 8 * ic:8 * ic + 8])),
                    ((lambda ic=ic, qc=qc: xn2Tl_sb[:, 2048 * ic + 128 * qc:
                                                    2048 * ic + 128 * qc + 128]),
                     (lambda ic=ic: wrTh_sb[:, 8 * ic:8 * ic + 8]))]:
                mms.append((lf, rf, [t_wrT], False, PB(4)[:, 0:8]))
        g = mm_group(4, mms)
        tmx = p.do("vector", lambda e: e.reduce_max(zcol[:], PB(4)[:, 0:8], AX.X, negate=True),
                   deps=[g, LU.get("zcol")])
        bk.read(4, tmx)
        tex = p.do("scalar", lambda e: e.activation(
            ez_sb[:], PB(4)[:, 0:8], AF.Exp, bias=zcol[:], scale=1.0, accum_out=zcol[:]),
            deps=[tmx, LU.get("ez")])
        bk.read(4, tex)
        trc = p.do("vector", lambda e: e.reciprocal(zcol[:], zcol[:]), deps=[tex])
        tp7 = p.do("vector", lambda e, qc=qc: e.tensor_tensor(
            p7_sb[:, qc:qc + 1], ez_sb[:, 7:8], zcol[:], Op.mult), deps=[trc])
        LU["zcol"] = tp7
        LU["ez"] = tp7

    p.barrier()
    rstack.close()
    w1h_sb = [dsb(f"w1h_sb{i}", [128, 8 * 256], BF16) for i in range(2)]
    w1l_sb = [dsb(f"w1l_sb{i}", [128, 8 * 256], BF16) for i in range(2)]
    w2h_sb = [dsb(f"w2h_sb{i}", [128, 2 * 1024], BF16) for i in range(2)]
    w2l_sb = [dsb(f"w2l_sb{i}", [128, 2 * 1024], BF16) for i in range(2)]
    h32 = ao_t[:, 0:512]
    hh_sb = dsb("hh_sb", [128, 4 * 512], BF16)
    hl_sb = dsb("hl_sb", [128, 4 * 512], BF16)
    load_brow(dstack, "b2", "biasD")

    tz = p.do("vector", lambda e: e.memset(y_acc[:], 0.0))
    yacc_tok = {qc: tz for qc in range(16)}

    for f in range(16):
        bufi = f % 2
        tw1 = []
        for ic in range(8):
            par = ic % 2
            tc = p.do("gpsimd", lambda e, f=f, ic=ic, bufi=bufi: e.dma_start(
                out=w1h_sb[bufi][:, 256 * ic:256 * ic + 256],
                in_=win["w1"][128 * ic:128 * ic + 128, 256 * f:256 * f + 256]),
                deps=[LU.get(("wcast", par)), LU.get(("w1", bufi))], dma=f"wc{par}")
            tf = p.do("sync", lambda e, f=f, ic=ic, par=par: e.dma_start(
                out=wscr[par][:, 0:256],
                in_=win["w1"][128 * ic:128 * ic + 128, 256 * f:256 * f + 256]),
                deps=[LU.get(("wscr", par)), LU.get(("w1", bufi))], dma=f"wf{par}")
            tt = p.do("vector", lambda e, ic=ic, bufi=bufi, par=par: e.scalar_tensor_tensor(
                w1l_sb[bufi][:, 256 * ic:256 * ic + 256], wscr[par][:, 0:256], 1.0,
                w1h_sb[bufi][:, 256 * ic:256 * ic + 256], Op.mult, Op.subtract),
                deps=[tc, tf])
            LU[("wcast", par)] = tt
            LU[("wscr", par)] = tt
            tw1.append(tt)
        tw2 = []
        for fs in range(2):
            par = fs
            tc = p.do("gpsimd", lambda e, f=f, fs=fs, bufi=bufi: e.dma_start(
                out=w2h_sb[bufi][:, 1024 * fs:1024 * fs + 1024],
                in_=win["w2"][256 * f + 128 * fs:256 * f + 128 * fs + 128, :]),
                deps=[LU.get(("wcast", par)), LU.get(("w2", bufi))], dma=f"wc{par}")
            tf = p.do("sync", lambda e, f=f, fs=fs, par=par: e.dma_start(
                out=wscr[par][:, 0:1024],
                in_=win["w2"][256 * f + 128 * fs:256 * f + 128 * fs + 128, :]),
                deps=[LU.get(("wscr", par)), LU.get(("w2", bufi))], dma=f"wf{par}")
            tt = p.do("vector", lambda e, fs=fs, bufi=bufi, par=par: e.scalar_tensor_tensor(
                w2l_sb[bufi][:, 1024 * fs:1024 * fs + 1024], wscr[par][:, 0:1024], 1.0,
                w2h_sb[bufi][:, 1024 * fs:1024 * fs + 1024], Op.mult, Op.subtract),
                deps=[tc, tf])
            LU[("wcast", par)] = tt
            LU[("wscr", par)] = tt
            tw2.append(tt)
        ylast = None

        def emit_w2(pendw):
            qtp, h_toksp = pendw
            slp = qtp % 2
            yl = None
            for qs in range(4):
                qc = 4 * qtp + qs
                for oh in range(2):
                    bkid = 2 * (qs % 2) + oh
                    mms = []
                    for fs in range(2):
                        mms += p3(
                            (lambda fs=fs, qs=qs, slp=slp: hh_sb[
                                :, 1024 * slp + 512 * fs + 128 * qs:
                                1024 * slp + 512 * fs + 128 * qs + 128]),
                            (lambda fs=fs, qs=qs, slp=slp: hl_sb[
                                :, 1024 * slp + 512 * fs + 128 * qs:
                                1024 * slp + 512 * fs + 128 * qs + 128]),
                            (lambda fs=fs, oh=oh, bufi=bufi: w2h_sb[bufi][
                                :, 1024 * fs + 512 * oh: 1024 * fs + 512 * oh + 512]),
                            (lambda fs=fs, oh=oh, bufi=bufi: w2l_sb[bufi][
                                :, 1024 * fs + 512 * oh: 1024 * fs + 512 * oh + 512]),
                            [h_toksp[fs], tw2[-1]])
                    if f == 15:
                        mms.append(((lambda: ones8[0:1, :]),
                                    (lambda oh=oh: brow["b2"][:, 512 * oh:512 * oh + 512]),
                                    [t_ones, t_b["b2"]], False, None))
                    g = mm_group(bkid, mms)
                    ty = p.do("vector", lambda e, qc=qc, oh=oh, bkid=bkid: e.tensor_tensor(
                        y_acc[:, 1024 * qc + 512 * oh: 1024 * qc + 512 * oh + 512],
                        y_acc[:, 1024 * qc + 512 * oh: 1024 * qc + 512 * oh + 512],
                        PB(bkid)[:, 0:512], Op.add),
                        deps=[g, yacc_tok[qc]])
                    bk.read(bkid, ty)
                    yacc_tok[qc] = ty
                    yl = ty
            LU[("h", slp)] = yl
            return yl

        pendw = None
        for qt in range(4):
            h_toks = []
            sl = qt % 2
            for fs in range(2):
                bkid = 4 + fs
                mms = []
                for ic in range(8):
                    mms += p3(
                        (lambda ic=ic, fs=fs, bufi=bufi: w1h_sb[bufi][
                            :, 256 * ic + 128 * fs: 256 * ic + 128 * fs + 128]),
                        (lambda ic=ic, fs=fs, bufi=bufi: w1l_sb[bufi][
                            :, 256 * ic + 128 * fs: 256 * ic + 128 * fs + 128]),
                        (lambda ic=ic, qt=qt: xn2Th_sb[:, 2048 * ic + 512 * qt:
                                                       2048 * ic + 512 * qt + 512]),
                        (lambda ic=ic, qt=qt: xn2Tl_sb[:, 2048 * ic + 512 * qt:
                                                       2048 * ic + 512 * qt + 512]),
                        [tw1[-1]])
                g = mm_group(bkid, mms)
                tg = p.do("scalar", lambda e, fs=fs, f=f, bkid=bkid: e.activation(
                    h32, PB(bkid)[:, 0:512], AF.Gelu,
                    bias=bcol["b1"][:, 2 * f + fs:2 * f + fs + 1], scale=1.0),
                    deps=[g, t_b["b1"], LU.get("h32")])
                bk.read(bkid, tg)
                thc = p.do("scalar", lambda e, fs=fs, sl=sl: e.copy(
                    hh_sb[:, 1024 * sl + 512 * fs:1024 * sl + 512 * fs + 512], h32),
                    deps=[tg, LU.get(("h", sl))])
                tlc = p.do("vector", lambda e, fs=fs, sl=sl: e.scalar_tensor_tensor(
                    hl_sb[:, 1024 * sl + 512 * fs:1024 * sl + 512 * fs + 512], h32, 1.0,
                    hh_sb[:, 1024 * sl + 512 * fs:1024 * sl + 512 * fs + 512],
                    Op.mult, Op.subtract),
                    deps=[thc])
                LU["h32"] = tlc
                h_toks.append(tlc)
            if pendw is not None:
                ylast = emit_w2(pendw)
            pendw = (qt, h_toks)
        ylast = emit_w2(pendw)
        LU[("w1", bufi)] = ylast
        LU[("w2", bufi)] = ylast

    LU["ao"] = LU.get("h32")
    for qc in range(16):
        tyb = p.do("vector", lambda e, qc=qc: e.tensor_scalar(
            ao_t[:], y_acc[:, 1024 * qc:1024 * qc + 1024], p7_sb[:, qc:qc + 1], None, Op.mult),
            deps=[yacc_tok[qc], LU.get("ao")])
        LU["ao"] = tyb
        ts = ln_stats(tyb, ao_t[:])
        tln = p.do("scalar", lambda e: e.activation(
            lnout[:], ao_t[:], AF.Identity, bias=ln_m[:], scale=ln_c[3][:]),
            deps=[ts, tyb, LU.get("lnout")])
        LU["ao"] = tln
        trl = p.do("sync", lambda e, qc=qc: e.dma_start(
            out=res_t[:], in_=res_hbm[128 * qc:128 * qc + 128, :]),
            deps=[LU.get("res_t"), LU.get("out_last")], dma="res_rl")
        tout = p.do("vector", lambda e: e.tensor_tensor(res_t[:], res_t[:], lnout[:], Op.add),
                    deps=[trl, tln])
        LU["res_t"] = tout
        LU["lnout"] = tout
        LU["out_last"] = p.do("sync", lambda e, qc=qc: e.dma_start(
            out=out_hbm[128 * qc:128 * qc + 128, :], in_=res_t[:]), deps=[tout], dma="out")

    dstack.close()
    cdstack.close()

    return _finish()


_cache = {}


def kernel(**kw):
    x = np.asarray(kw["L0_attn_x"], np.float32)
    if "nc" not in _cache:
        _cache["nc"] = build_program()
    nc = _cache["nc"]
    shared = {
        "wdq": kw["L0_attn_Wd_q_w"], "wuq": kw["L0_attn_Wu_q_w"],
        "wqr": kw["L0_attn_Wq_r_w"], "wkr": kw["L0_attn_Wk_r_w"],
        "wdkv": kw["L0_attn_Wd_kv_w"], "wuk": kw["L0_attn_Wu_k_w"],
        "wuv": kw["L0_attn_Wu_v_w"], "wo": kw["L0_attn_Wo_w"],
        "wr": kw["L0_ffn_Wr"], "w1": kw["L0_ffn_W1"][7], "w2": kw["L0_ffn_W2"][7],
        "bdq": kw["L0_attn_Wd_q_b"], "buq": kw["L0_attn_Wu_q_b"],
        "bqr": kw["L0_attn_Wq_r_b"], "bkr": kw["L0_attn_Wk_r_b"],
        "bdkv": kw["L0_attn_Wd_kv_b"], "buk": kw["L0_attn_Wu_k_b"],
        "buv": kw["L0_attn_Wu_v_b"], "bo": kw["L0_attn_Wo_b"],
        "b1": kw["L0_ffn_b1"][7], "b2": kw["L0_ffn_b2"][7],
    }
    shared = {k: np.ascontiguousarray(np.asarray(v, np.float32)) for k, v in shared.items()}
    in_maps = []
    for core in range(8):
        b, hf = core // 2, core % 2
        q0 = NQ * hf
        xvv = np.ascontiguousarray(np.concatenate([x[b, q0:], x[b, :q0]], axis=0))
        m = dict(shared)
        m["xv"] = xvv
        m["seg"] = np.array([q0, (q0 + NQ) % S], np.float32)
        in_maps.append(m)
    res = run_bass_kernel_spmd(nc, in_maps, core_ids=list(range(8)))
    r0 = np.asarray(res.results[0]["out"])
    out = np.empty((B, S, D), r0.dtype)
    for core in range(8):
        b, hf = core // 2, core % 2
        out[b, NQ * hf:NQ * hf + NQ] = np.asarray(res.results[core]["out"])
    return out


# version 18
# speedup vs baseline: 1.0484x; 1.0484x over previous
import contextlib
import math
import numpy as np
import concourse.bass as bass
import concourse.mybir as mybir
from concourse.bass_utils import run_bass_kernel_spmd
from concourse.alu_op_type import AluOpType as Op

F32 = mybir.dt.float32
BF16 = mybir.dt.bfloat16
AF = mybir.ActivationFunctionType
AX = mybir.AxisListType

B, S, D = 4, 4096, 1024
NQ = 2048
H = 16
DQ = DKV = 512
FF = 4096
EPS = 1e-5
SCALE = float(np.sqrt(np.float32(H + DQ + DKV)))
INV_SCALE = 1.0 / SCALE
PI = math.pi
TWO_PI = 2 * math.pi
HALF_PI = 0.5 * math.pi
HALF_PI_F32 = float(np.float32(0.5 * math.pi))
HALF_PI_LO = HALF_PI - HALF_PI_F32
LN1E4 = 9.210340371976184

ENGS = ("sync", "tensor", "vector", "scalar", "gpsimd")


class Prog:

    def __init__(self):
        self.steps = []
        self.barrier_toks = []

    def do(self, eng, fn, deps=(), dma=None):
        t = len(self.steps)
        dd = [d for d in deps if d is not None]
        dd.extend(self.barrier_toks)
        self.steps.append((eng, fn, tuple(dd), dma))
        return t

    def barrier(self):
        last = {}
        for i, (eng, _f, _d, dma) in enumerate(self.steps):
            last[eng if dma is None else ("ch", dma)] = i
        self.barrier_toks = list(last.values())

    def emit(self, nc, gstack, block, csem):
        steps = self.steps
        need = [False] * len(steps)
        for (_e, _f, deps, _dma) in steps:
            for d in deps:
                need[d] = True
        chans = []
        for (_e, _f, _d, dma) in steps:
            if dma is not None and dma not in chans:
                chans.append(dma)
        chsem = {c: gstack.enter_context(nc.semaphore("ch_" + c)) for c in chans}
        meta = [None] * len(steps)
        ccount = {e: 0 for e in ENGS}
        chcount = {c: 0 for c in chans}
        eng_chfinal = {e: {} for e in ENGS}
        for i, (eng, _f, _deps, dma) in enumerate(steps):
            if dma is not None:
                chcount[dma] += 16
                meta[i] = ("ch", dma, chcount[dma])
                eng_chfinal[eng][dma] = chcount[dma]
            else:
                if need[i]:
                    ccount[eng] += 1
                meta[i] = ("c", eng, ccount[eng])

        def body_for(E):
            mine = [(i, steps[i]) for i in range(len(steps)) if steps[i][0] == E]

            def body(engobj):
                ensured = {}
                for i, (_e, fn, deps, dma) in mine:
                    for d in deps:
                        kind, key2, dval = meta[d]
                        key = (kind, key2)
                        if ensured.get(key, 0) >= dval:
                            continue
                        sem = csem[key2] if kind == "c" else chsem[key2]
                        engobj.wait_ge(sem, dval)
                        ensured[key] = dval
                    inst = fn(engobj)
                    if dma is not None:
                        inst.then_inc(chsem[dma], 16)
                    elif need[i]:
                        inst.then_inc(csem[E], 1)
                for c, v in eng_chfinal[E].items():
                    engobj.wait_ge(chsem[c], v)

            return body

        for E in ENGS:
            getattr(block, E)(body_for(E))


class Banks:
    def __init__(self):
        self.readers = {b: [] for b in range(8)}
        self.writer = {b: None for b in range(8)}

    def pre_write(self, b):
        r = list(self.readers[b])
        if self.writer[b] is not None:
            r.append(self.writer[b])
        return r

    def wrote(self, b, tok):
        self.writer[b] = tok
        self.readers[b] = []

    def read(self, b, tok):
        self.readers[b].append(tok)


def build_program(dump=(), cut=None):
    nc = bass.Bass("TRN2", target_bir_lowering=False)
    p = Prog()
    bk = Banks()
    LU = {}

    xv = nc.dram_tensor("xv", [S, D], F32, kind="ExternalInput").ap()
    seg = nc.dram_tensor("seg", [2], F32, kind="ExternalInput").ap()
    win = {}
    for nm, shp in [("wdq", [D, DQ]), ("wuq", [DQ, D]), ("wqr", [DQ, D]),
                    ("wkr", [D, D]), ("wdkv", [D, DKV]), ("wuk", [DKV, D]),
                    ("wuv", [DKV, D]), ("wo", [D, D]), ("wr", [8, D]),
                    ("w1", [D, FF]), ("w2", [FF, D]),
                    ("bdq", [DQ]), ("buq", [D]), ("bqr", [D]), ("bkr", [D]),
                    ("bdkv", [DKV]), ("buk", [D]), ("buv", [D]), ("bo", [D]),
                    ("b1", [FF]), ("b2", [D])]:
        win[nm] = nc.dram_tensor(nm, shp, F32, kind="ExternalInput").ap()
    out_hbm = nc.dram_tensor("out", [NQ, D], F32, kind="ExternalOutput").ap()

    dumps = {}

    def dump_t(nm, shp):
        if nm in dump:
            dumps[nm] = nc.dram_tensor("dump_" + nm, shp, F32, kind="ExternalOutput").ap()
        return dumps.get(nm)

    xnT_hbm = nc.dram_tensor("xnT_hbm", [D, 2, S], BF16, kind="Internal").ap()
    kT_hbm = nc.dram_tensor("kT_hbm", [D, S // 128, 2, 128], BF16, kind="Internal").ap()
    v_hbm = nc.dram_tensor("v_hbm", [S, 2, H * 65], BF16, kind="Internal").ap()
    qT_hbm = nc.dram_tensor("qT_hbm", [D, 2, NQ], BF16, kind="Internal").ap()
    res_hbm = nc.dram_tensor("res_hbm", [NQ, D], F32, kind="Internal").ap()

    d_res = dump_t("res", [NQ, D])
    d_negm = dump_t("negm", [1, 1])
    d_OT = dump_t("OT", [D, NQ])

    gstack = contextlib.ExitStack()

    def mk_sb(stk):
        return lambda nm, shp, dt=F32: stk.enter_context(nc.sbuf_tensor(nm, shp, dt)).ap()

    sb = mk_sb(gstack)

    ident = sb("ident", [128, 128])
    ones8 = sb("ones8", [8, 128])
    halfpi_col = sb("halfpi_col", [128, 1])
    segcol = sb("segcol", [128, 2])
    seg_sb = sb("seg_sb", [1, 2])
    negM = sb("negM", [128, 1])
    e_sb = sb("e_sb", [128, 8 * 16])
    kmax_c = sb("kmax_c", [16, 1])
    qmax_c = sb("qmax_c", [16, 1])
    t16 = sb("t16", [16, 1])
    p7_sb = sb("p7_sb", [128, 16])
    bcol = {}
    for nm, w in [("bdq", 4), ("bdkv", 4), ("buq", 8), ("buk", 8), ("b1", 32)]:
        bcol[nm] = sb(nm + "_col", [128, w])
    buqs = sb("buqs_col", [128, 8])
    ln_c = [sb(f"ln_c{i}", [128, 1]) for i in range(4)]
    ln_m = sb("ln_m", [128, 1])
    sq_scr = sb("sq_scr", [128, 1024])
    lnout = sb("lnout", [128, 1024])
    mrow = sb("mrow", [1, 64])
    xt = [sb("xt0", [128, 1024]), None]
    xn = sb("xn", [128, 1024])
    res_t = sb("res_t", [128, 1024])
    ao_t = sb("ao_t", [128, 1024])
    wscr = [sb(f"wscr{i}", [128, 1024]) for i in range(2)]

    psum = gstack.enter_context(nc.psum_tensor("pall", [128, 4096], F32)).ap()
    PB = lambda b: psum[:, 512 * b:512 * (b + 1)]

    csem = {e: gstack.enter_context(nc.semaphore("c_" + e)) for e in ENGS}

    def _finish():
        if cut is not None:
            with nc.Block() as block:
                p.emit(nc, gstack, block, csem)
            return nc
        with gstack:
            with nc.Block() as block:
                p.emit(nc, gstack, block, csem)
        return nc

    t_id0 = p.do("gpsimd", lambda e: e.memset(ident[:], 0.0))
    t_id = p.do("gpsimd", lambda e: e.affine_select(
        out=ident[:], in_=ident[:], compare_op=Op.not_equal, fill=1.0,
        base=0, pattern=[[-1, 128]], channel_multiplier=1), deps=[t_id0])
    t_ones = p.do("gpsimd", lambda e: e.memset(ones8[:], 1.0))
    t_hp0 = p.do("gpsimd", lambda e: e.memset(halfpi_col[:], HALF_PI_F32))
    t_hp = p.do("vector", lambda e: e.tensor_scalar_add(halfpi_col[:], halfpi_col[:],
                                                        HALF_PI_LO), deps=[t_hp0])
    t_kin = p.do("gpsimd", lambda e: e.memset(kmax_c[:], -3.0e38))
    t_qin = p.do("gpsimd", lambda e: e.memset(qmax_c[:], -3.0e38))
    t_e0 = p.do("gpsimd", lambda e: e.memset(e_sb[:], 0.0))
    t_es = []
    for ic in range(8):
        for sub in range(2):
            h = 2 * ic + sub
            t_es.append(p.do("gpsimd", lambda e, ic=ic, sub=sub, h=h: e.memset(
                e_sb[64 * sub:64 * sub + 64, 16 * ic + h:16 * ic + h + 1], 1.0),
                deps=[t_e0]))

    t_b = {}
    for nm in ["bdq", "bdkv", "buq", "buk", "b1"]:
        t_b[nm] = p.do("sync", lambda e, nm=nm: e.dma_start(
            out=bcol[nm][:], in_=win[nm].rearrange("(c p) -> p c", p=128),
            allow_slow_non_contiguous=True), dma="bias0")
    t_sg0 = p.do("sync", lambda e: e.dma_start(out=seg_sb[:], in_=seg.unsqueeze(0)),
                 dma="bias0")
    for nm in ["bdq", "bdkv", "buq", "buk", "b1"]:
        t_b[nm] = t_sg0
    t_sg1 = p.do("tensor", lambda e: e.matmul(
        PB(1)[:, 0:2], ones8[0:1, :], seg_sb[:], start=True, stop=True),
        deps=[t_sg0, t_ones] + bk.pre_write(1))
    bk.wrote(1, t_sg1)
    t_seg = p.do("vector", lambda e: e.tensor_copy(segcol[:], PB(1)[:, 0:2]), deps=[t_sg1])
    bk.read(1, t_seg)

    t_buqs = p.do("vector", lambda e: e.tensor_scalar_mul(buqs[:], bcol["buq"][:], INV_SCALE),
                  deps=[t_b["buq"]])

    brow = {}

    def load_brow(stk, nm, ch):
        t = mk_sb(stk)(nm + "_row", [1, D])
        brow[nm] = t
        t_b[nm] = p.do("sync", lambda e, nm=nm, t=t: e.dma_start(
            out=t[:], in_=win[nm].unsqueeze(0)), dma=ch)

    wsb = {}

    def load_w_hl(stk, nm, I, O):
        s = mk_sb(stk)
        th = s(nm + "_h", [128, (I // 128) * O], BF16)
        tl = s(nm + "_l", [128, (I // 128) * O], BF16)
        toks = []
        for ic in range(I // 128):
            par = ic % 2
            tc = p.do("gpsimd", lambda e, ic=ic, th=th, nm=nm, O=O: e.dma_start(
                out=th[:, O * ic:O * (ic + 1)], in_=win[nm][128 * ic:128 * (ic + 1), :]),
                deps=[LU.get(("wcast", par))], dma=f"wc{par}")
            tf = p.do("sync", lambda e, ic=ic, nm=nm, O=O, par=par: e.dma_start(
                out=wscr[par][:, 0:O], in_=win[nm][128 * ic:128 * (ic + 1), :]),
                deps=[LU.get(("wscr", par))], dma=f"wf{par}")
            tt = p.do("vector", lambda e, ic=ic, th=th, tl=tl, O=O, par=par:
                      e.scalar_tensor_tensor(
                          tl[:, O * ic:O * (ic + 1)], wscr[par][:, 0:O], 1.0,
                          th[:, O * ic:O * (ic + 1)], Op.mult, Op.subtract),
                      deps=[tc, tf])
            LU[("wcast", par)] = tt
            LU[("wscr", par)] = tt
            toks.append(tt)
        wsb[nm] = (th, tl, O)
        return toks

    WH = lambda nm, ic, lo, hi: wsb[nm][0][:, wsb[nm][2] * ic + lo: wsb[nm][2] * ic + hi]
    WL = lambda nm, ic, lo, hi: wsb[nm][1][:, wsb[nm][2] * ic + lo: wsb[nm][2] * ic + hi]

    def mm_group(bank, mms, extra_deps=()):
        pre = bk.pre_write(bank) + list(extra_deps)
        tok = None
        nmm = len(mms)
        for j, (lf, rf, dd, istr, osl) in enumerate(mms):
            o = PB(bank)[:, 0:512] if osl is None else osl
            tok = p.do("tensor", lambda e, lf=lf, rf=rf, o=o, j=j, istr=istr, nmm=nmm: e.matmul(
                o, lf(), rf(), start=(j == 0), stop=(j == nmm - 1),
                is_transpose=True if istr else None),
                deps=(pre if j == 0 else []) + list(dd))
        bk.wrote(bank, tok)
        return tok

    def p3(lh, ll, rh, rl, dd):
        return [(lh, rh, dd, False, None), (lh, rl, dd, False, None),
                (ll, rh, dd, False, None)]

    def ln_stats(src_tok, src_ap, dep_extra=()):
        t1 = p.do("scalar", lambda e: e.activation(
            sq_scr[:], src_ap, AF.Identity, bias=0.0, scale=1.0, accum_out=ln_c[0][:]),
            deps=[src_tok, LU.get("sq_scr")] + list(dep_extra))
        t2 = p.do("scalar", lambda e: e.activation(
            sq_scr[:], src_ap, AF.Square, bias=0.0, scale=1.0, accum_out=ln_c[2][:]),
            deps=[src_tok, t1] + list(dep_extra))
        LU["sq_scr"] = t2
        t3 = p.do("vector", lambda e: e.tensor_scalar_mul(ln_c[1][:], ln_c[0][:], 1.0 / 1024.0),
                  deps=[t1])
        t4 = p.do("vector", lambda e: e.tensor_scalar_mul(ln_c[2][:], ln_c[2][:], 1.0 / 1024.0),
                  deps=[t2])
        t5 = p.do("vector", lambda e: e.tensor_tensor(ln_m[:], ln_c[1][:], ln_c[1][:], Op.mult),
                  deps=[t3])
        t6 = p.do("vector", lambda e: e.tensor_tensor(ln_c[2][:], ln_c[2][:], ln_m[:],
                                                      Op.subtract), deps=[t4, t5])
        t7 = p.do("vector", lambda e: e.tensor_scalar_add(ln_c[2][:], ln_c[2][:], EPS), deps=[t6])
        t8 = p.do("scalar", lambda e: e.activation(ln_c[3][:], ln_c[2][:], AF.Sqrt,
                                                   bias=0.0, scale=1.0), deps=[t7])
        t9 = p.do("vector", lambda e: e.reciprocal(ln_c[3][:], ln_c[3][:]), deps=[t8])
        t10 = p.do("vector", lambda e: e.tensor_tensor(ln_m[:], ln_c[1][:], ln_c[3][:], Op.mult),
                   deps=[t9])
        t11 = p.do("vector", lambda e: e.tensor_scalar_mul(ln_m[:], ln_m[:], -1.0), deps=[t10])
        return t11

    astack = contextlib.ExitStack()
    asb = mk_sb(astack)
    xt[1] = asb("xt1", [128, 1024])
    theta_bc = asb("theta_bc", [128, 512])
    t_th0 = p.do("gpsimd", lambda e: e.iota(
        theta_bc[0:1, :], pattern=[[1, 512]], base=0, channel_multiplier=0,
        allow_small_or_imprecise_dtypes=True))
    t_th1 = p.do("scalar", lambda e: e.activation(
        theta_bc[0:1, :], theta_bc[0:1, :], AF.Exp, bias=0.0, scale=-2.0 * LN1E4),
        deps=[t_th0])
    t_th2 = p.do("vector", lambda e: e.tensor_scalar_mul(
        theta_bc[0:1, :], theta_bc[0:1, :], 1.0 / 512.0), deps=[t_th1])
    t_th3 = p.do("tensor", lambda e: e.matmul(
        PB(0)[:, 0:512], ones8[0:1, :], theta_bc[0:1, :], start=True, stop=True),
        deps=[t_ones, t_th2] + bk.pre_write(0))
    bk.wrote(0, t_th3)
    t_th = p.do("vector", lambda e: e.tensor_copy(theta_bc[:], PB(0)[:, 0:512]),
                deps=[t_th3])
    bk.read(0, t_th)
    xnTh_t = asb("xnTh_t", [128, 8 * 512], BF16)
    xnTl_t = asb("xnTl_t", [128, 8 * 512], BF16)
    ckvh_t = asb("ckvh_t", [128, 4 * 512], BF16)
    ckvl_t = asb("ckvl_t", [128, 4 * 512], BF16)
    outh_t = asb("outh_t", [128, 8 * 512], BF16)
    outl_t = asb("outl_t", [128, 8 * 512], BF16)
    big_t = asb("big_t", [128, 4096])
    vh_t = asb("vh_t", [128, 2 * 1040], BF16)
    vl_t = asb("vl_t", [128, 2 * 1040], BF16)
    cos_t = asb("cos_t", [128, 512])
    sin_t = asb("sin_t", [128, 512])
    pos_c = asb("pos_c", [128, 1])
    rr_t = asb("rr_t", [128, 512])
    k12_t = asb("k12_t", [128, 512])
    tmpa = asb("tmpa", [128, 512])
    tmpb = asb("tmpb", [128, 512])

    def cos_sin(r0):
        base = r0 % 2048
        si = r0 // 2048
        ta = p.do("gpsimd", lambda e, base=base: e.iota(
            pos_c[:], pattern=[[0, 1]], base=base, channel_multiplier=1,
            allow_small_or_imprecise_dtypes=True), deps=[LU.get("pos")])
        tb = p.do("vector", lambda e, si=si: e.tensor_tensor(
            pos_c[:], pos_c[:], segcol[:, si:si + 1], Op.add), deps=[ta, t_seg])
        tr = p.do("vector", lambda e: e.tensor_scalar(
            rr_t[:], theta_bc[:], pos_c[:], None, Op.mult), deps=[tb, t_th, LU.get("rr")])
        tk1 = p.do("vector", lambda e: e.tensor_scalar(k12_t[:], rr_t[:], PI, None, Op.is_ge),
                   deps=[tr, LU.get("k12")])
        tk1b = p.do("vector", lambda e: e.tensor_scalar_mul(k12_t[:], k12_t[:], TWO_PI),
                    deps=[tk1])
        tk1c = p.do("vector", lambda e: e.tensor_tensor(tmpa[:], rr_t[:], k12_t[:], Op.subtract),
                    deps=[tk1b, LU.get("tmpa")])
        tsin = p.do("scalar", lambda e: e.activation(sin_t[:], tmpa[:], AF.Sin,
                                                     bias=0.0, scale=1.0),
                    deps=[tk1c, LU.get("sin_w")])
        tq1 = p.do("vector", lambda e: e.tensor_scalar(k12_t[:], rr_t[:], HALF_PI, None,
                                                       Op.is_ge), deps=[tr, tk1c])
        tq2 = p.do("vector", lambda e: e.tensor_scalar(tmpb[:], rr_t[:], 5.0 * HALF_PI, None,
                                                       Op.is_ge), deps=[tr, LU.get("tmpb")])
        tq3 = p.do("vector", lambda e: e.tensor_tensor(k12_t[:], k12_t[:], tmpb[:], Op.add),
                   deps=[tq1, tq2])
        tq4 = p.do("vector", lambda e: e.tensor_scalar_mul(k12_t[:], k12_t[:], TWO_PI),
                   deps=[tq3])
        tq5 = p.do("vector", lambda e: e.tensor_tensor(tmpb[:], rr_t[:], k12_t[:], Op.subtract),
                   deps=[tq4])
        tcos = p.do("scalar", lambda e: e.activation(cos_t[:], tmpb[:], AF.Sin,
                                                     bias=halfpi_col[:], scale=1.0),
                    deps=[tq5, t_hp, LU.get("cos_w")])
        LU["rr"] = tq5
        LU["k12"] = tq5
        LU["tmpa"] = tsin
        LU["tmpb"] = tcos
        LU["pos"] = tr
        return tcos, tsin

    def rope_apply(ps_ap, oh, dst_ap, dep_group, tcos, tsin):
        co = cos_t[:, 256 * oh:256 * oh + 256]
        si = sin_t[:, 256 * oh:256 * oh + 256]
        ev, od = ps_ap[:, 0:512:2], ps_ap[:, 1:512:2]
        dev, dod = dst_ap[:, 0:512:2], dst_ap[:, 1:512:2]
        t1 = p.do("vector", lambda e: e.tensor_tensor(tmpa[:, 0:256], ev, co, Op.mult),
                  deps=[dep_group, tcos, LU.get("tmpa")])
        t2 = p.do("vector", lambda e: e.tensor_tensor(tmpb[:, 0:256], od, si, Op.mult),
                  deps=[dep_group, tsin, LU.get("tmpb")])
        t3 = p.do("vector", lambda e: e.tensor_tensor(dev, tmpa[:, 0:256], tmpb[:, 0:256],
                                                      Op.subtract), deps=[t1, t2])
        t4 = p.do("vector", lambda e: e.tensor_tensor(tmpa[:, 0:256], od, co, Op.mult),
                  deps=[dep_group, t3])
        t5 = p.do("vector", lambda e: e.tensor_tensor(tmpb[:, 0:256], ev, si, Op.mult),
                  deps=[dep_group, t3])
        t6 = p.do("vector", lambda e: e.tensor_tensor(dod, tmpa[:, 0:256], tmpb[:, 0:256],
                                                      Op.add), deps=[t4, t5])
        LU["tmpa"] = t6
        LU["tmpb"] = t6
        return t6, [t1, t2, t4, t5]

    def x4(t):
        return t.rearrange("p (jj c2 k) -> p jj c2 k", c2=4, k=128)

    def ln_transpose_tile(r0_tile, spill):
        toks_all = []
        for c in range(4):
            buf = xt[c % 2]
            r0 = r0_tile + 128 * c
            tl = p.do("sync", lambda e, buf=buf, r0=r0: e.dma_start(
                out=buf[:], in_=xv[r0:r0 + 128, :]), deps=[LU.get(("xt", c % 2))],
                dma=f"xt{c % 2}")
            ts = ln_stats(tl, buf[:])
            txn = p.do("scalar", lambda e, buf=buf: e.activation(
                xn[:], buf[:], AF.Identity, bias=ln_m[:], scale=ln_c[3][:]),
                deps=[ts, LU.get("xn")])
            LU[("xt", c % 2)] = txn
            ev_last = None
            for half in range(2):
                bkid = half
                mms = []
                for jj in range(4):
                    j = 4 * half + jj
                    mms.append(((lambda j=j: xn[:, 128 * j:128 * j + 128]),
                                (lambda: ident[:]),
                                [txn, t_id], True, PB(bkid)[:, 128 * jj:128 * jj + 128]))
                g = mm_group(bkid, mms)
                th_ = p.do("vector", lambda e, bkid=bkid, half=half, c=c: e.tensor_copy(
                    x4(xnTh_t)[:, 4 * half:4 * half + 4, c, :],
                    PB(bkid)[:, 0:512].rearrange("p (j k) -> p j k", k=128)),
                    deps=[g, LU.get("xnT_t"), LU.get("xnT_sp_last")])
                bk.read(bkid, th_)
                tl_ = p.do("vector", lambda e, bkid=bkid, half=half, c=c:
                           e.scalar_tensor_tensor(
                    x4(xnTl_t)[:, 4 * half:4 * half + 4, c, :],
                    PB(bkid)[:, 0:512].rearrange("p (j k) -> p j k", k=128), 1.0,
                    x4(xnTh_t)[:, 4 * half:4 * half + 4, c, :],
                    Op.mult, Op.subtract), deps=[g, th_])
                bk.read(bkid, tl_)
                ev_last = tl_
                toks_all.append(th_)
                toks_all.append(tl_)
            LU["xn"] = ev_last
        if spill:
            for j in range(8):
                LU["xnT_sp_last"] = p.do("sync", lambda e, j=j, r0_tile=r0_tile: e.dma_start(
                    out=xnT_hbm[128 * j:128 * j + 128, 0, r0_tile:r0_tile + 512],
                    in_=xnTh_t[:, 512 * j:512 * j + 512]), deps=list(toks_all),
                    dma="xnT_sp")
                LU["xnT_sp_last"] = p.do("sync", lambda e, j=j, r0_tile=r0_tile: e.dma_start(
                    out=xnT_hbm[128 * j:128 * j + 128, 1, r0_tile:r0_tile + 512],
                    in_=xnTl_t[:, 512 * j:512 * j + 512]), deps=list(toks_all),
                    dma="xnT_sp")
        return toks_all

    wa1 = contextlib.ExitStack()
    load_brow(wa1, "bkr", "biasA1")
    load_brow(wa1, "buv", "biasA1")
    t_b["bkr"] = t_b["buv"]
    tw_dkv = load_w_hl(wa1, "wdkv", D, DKV)
    tw_kr = load_w_hl(wa1, "wkr", D, D)
    tw_uk = load_w_hl(wa1, "wuk", DKV, D)
    tw_uv = load_w_hl(wa1, "wuv", DKV, D)

    vones = []
    t_vl0 = p.do("gpsimd", lambda e: e.memset(vl_t[:], 0.0))
    for c in range(2):
        for hg in range(16):
            vones.append(p.do("gpsimd", lambda e, c=c, hg=hg: e.memset(
                vh_t[:, 1040 * c + 65 * hg + 64: 1040 * c + 65 * hg + 65], 1.0)))
    t_vones = vones[-1]

    tkm = None
    for st in range(8):
        r0_tile = 512 * st
        xnT_toks = ln_transpose_tile(r0_tile, spill=True)
        ckv_evac = []
        for oc in range(4):
            bkid = 2 + (oc % 2)
            mms = []
            for ic in range(8):
                mms += p3((lambda ic=ic, oc=oc: WH("wdkv", ic, 128 * oc, 128 * oc + 128)),
                          (lambda ic=ic, oc=oc: WL("wdkv", ic, 128 * oc, 128 * oc + 128)),
                          (lambda ic=ic: xnTh_t[:, 512 * ic:512 * ic + 512]),
                          (lambda ic=ic: xnTl_t[:, 512 * ic:512 * ic + 512]),
                          [tw_dkv[ic], xnT_toks[-1]])
            g = mm_group(bkid, mms)
            tvh = p.do("scalar", lambda e, oc=oc, bkid=bkid: e.activation(
                ckvh_t[:, 512 * oc:512 * oc + 512], PB(bkid)[:, 0:512], AF.Identity,
                bias=bcol["bdkv"][:, oc:oc + 1], scale=1.0),
                deps=[g, t_b["bdkv"], LU.get("ckvT")])
            bk.read(bkid, tvh)
            tvl_ = p.do("vector", lambda e, oc=oc, bkid=bkid: e.scalar_tensor_tensor(
                ckvl_t[:, 512 * oc:512 * oc + 512], PB(bkid)[:, 0:512],
                bcol["bdkv"][:, oc:oc + 1], ckvh_t[:, 512 * oc:512 * oc + 512],
                Op.add, Op.subtract), deps=[g, tvh])
            bk.read(bkid, tvl_)
            ckv_evac.append(tvl_)
        LU["ckvT"] = ckv_evac[-1]

        rope_done = []
        for c in range(4):
            tcos, tsin = cos_sin(r0_tile + 128 * c)
            for oh in range(2):
                bkid = 4 + oh
                mms = []
                for ic in range(8):
                    mms += p3(
                        (lambda ic=ic, c=c: xnTh_t[:, 512 * ic + 128 * c:
                                                   512 * ic + 128 * c + 128]),
                        (lambda ic=ic, c=c: xnTl_t[:, 512 * ic + 128 * c:
                                                   512 * ic + 128 * c + 128]),
                        (lambda ic=ic, oh=oh: WH("wkr", ic, 512 * oh, 512 * oh + 512)),
                        (lambda ic=ic, oh=oh: WL("wkr", ic, 512 * oh, 512 * oh + 512)),
                        [tw_kr[ic], xnT_toks[-1]])
                mms.append(((lambda: ones8[0:1, :]),
                            (lambda oh=oh: brow["bkr"][:, 512 * oh:512 * oh + 512]),
                            [t_ones, t_b["bkr"]], False, None))
                g = mm_group(bkid, mms, extra_deps=[LU.get(("krp", c, oh))])
                LU["xnT_t"] = g
                dst = big_t[:, 1024 * c + 512 * oh: 1024 * c + 512 * oh + 512]
                t6, readers = rope_apply(PB(bkid)[:, 0:512], oh, dst, g, tcos, tsin)
                for rt in readers + [t6]:
                    bk.read(bkid, rt)
                LU[("krp", c, oh)] = t6
                rope_done.append(t6)

        kt_evh = []
        for oc in range(8):
            bkid = 6 + (oc % 2)
            mms = []
            for cc in range(4):
                mms += p3((lambda cc=cc, oc=oc: WH("wuk", cc, 128 * oc, 128 * oc + 128)),
                          (lambda cc=cc, oc=oc: WL("wuk", cc, 128 * oc, 128 * oc + 128)),
                          (lambda cc=cc: ckvh_t[:, 512 * cc:512 * cc + 512]),
                          (lambda cc=cc: ckvl_t[:, 512 * cc:512 * cc + 512]),
                          [tw_uk[cc], ckv_evac[-1]])
            for c in range(4):
                mms.append(((lambda c=c, oc=oc: big_t[:, 1024 * c + 128 * oc:
                                                      1024 * c + 128 * oc + 128]),
                            (lambda: ident[:]),
                            [rope_done[-1], t_id], True, PB(bkid)[:, 128 * c:128 * c + 128]))
            g = mm_group(bkid, mms)
            th_ = p.do("scalar", lambda e, oc=oc, bkid=bkid: e.activation(
                outh_t[:, 512 * oc:512 * oc + 512], PB(bkid)[:, 0:512], AF.Identity,
                bias=bcol["buk"][:, oc:oc + 1], scale=1.0),
                deps=[g, t_b["buk"], LU.get(("outT", oc))])
            kt_evh.append(th_)
            bk.read(bkid, th_)
            tl_ = p.do("vector", lambda e, oc=oc, bkid=bkid: e.scalar_tensor_tensor(
                outl_t[:, 512 * oc:512 * oc + 512], PB(bkid)[:, 0:512],
                bcol["buk"][:, oc:oc + 1], outh_t[:, 512 * oc:512 * oc + 512],
                Op.add, Op.subtract), deps=[g, th_])
            bk.read(bkid, tl_)
            t1 = p.do("sync", lambda e, oc=oc, st=st: e.dma_start(
                out=kT_hbm[128 * oc:128 * oc + 128, 4 * st:4 * st + 4, 0, :],
                in_=outh_t[:, 512 * oc:512 * oc + 512]
                    .rearrange("p (kc k) -> p kc k", k=128)), deps=[th_, tl_],
                dma=f"osp{oc}")
            tsp = p.do("sync", lambda e, oc=oc, st=st: e.dma_start(
                out=kT_hbm[128 * oc:128 * oc + 128, 4 * st:4 * st + 4, 1, :],
                in_=outl_t[:, 512 * oc:512 * oc + 512]
                    .rearrange("p (kc k) -> p kc k", k=128)), deps=[th_, tl_, t1],
                dma=f"osp{oc}")
            LU[("outT", oc)] = tsp

        for c in range(4):
            cs = c % 2
            ev_t = []
            for oh in range(2):
                bkid = 4 + oh
                mms = []
                for cc in range(4):
                    mms += p3(
                        (lambda cc=cc, c=c: ckvh_t[:, 512 * cc + 128 * c:
                                                   512 * cc + 128 * c + 128]),
                        (lambda cc=cc, c=c: ckvl_t[:, 512 * cc + 128 * c:
                                                   512 * cc + 128 * c + 128]),
                        (lambda cc=cc, oh=oh: WH("wuv", cc, 512 * oh, 512 * oh + 512)),
                        (lambda cc=cc, oh=oh: WL("wuv", cc, 512 * oh, 512 * oh + 512)),
                        [tw_uv[cc], ckv_evac[-1]])
                mms.append(((lambda: ones8[0:1, :]),
                            (lambda oh=oh: brow["buv"][:, 512 * oh:512 * oh + 512]),
                            [t_ones, t_b["buv"]], False, None))
                g = mm_group(bkid, mms, extra_deps=[LU.get(("vev", cs, oh))])
                vh_dst = vh_t[:, 1040 * cs + 520 * oh: 1040 * cs + 520 * oh + 520] \
                    .rearrange("p (h j) -> p h j", j=65)[:, :, 0:64]
                vl_dst = vl_t[:, 1040 * cs + 520 * oh: 1040 * cs + 520 * oh + 520] \
                    .rearrange("p (h j) -> p h j", j=65)[:, :, 0:64]
                th_ = p.do("vector", lambda e, vh_dst=vh_dst, bkid=bkid: e.tensor_copy(
                    vh_dst, PB(bkid)[:, 0:512].rearrange("p (h j) -> p h j", j=64)),
                    deps=[g, t_vones, LU.get(("vsp", cs))])
                bk.read(bkid, th_)
                tl_ = p.do("vector", lambda e, vh_dst=vh_dst, vl_dst=vl_dst, bkid=bkid:
                           e.scalar_tensor_tensor(
                    vl_dst, PB(bkid)[:, 0:512].rearrange("p (h j) -> p h j", j=64), 1.0,
                    vh_dst, Op.mult, Op.subtract),
                    deps=[g, th_, t_vl0, LU.get(("vsp", cs))])
                bk.read(bkid, tl_)
                ev_t += [th_, tl_]
                LU[("vev", cs, oh)] = tl_
            tsp = p.do("sync", lambda e, cs=cs, c=c, r0_tile=r0_tile: e.dma_start(
                out=v_hbm[r0_tile + 128 * c: r0_tile + 128 * c + 128, 0, :],
                in_=vh_t[:, 1040 * cs:1040 * cs + 1040]), deps=ev_t, dma=f"vsp{cs}")
            tsp = p.do("sync", lambda e, cs=cs, c=c, r0_tile=r0_tile: e.dma_start(
                out=v_hbm[r0_tile + 128 * c: r0_tile + 128 * c + 128, 1, :],
                in_=vl_t[:, 1040 * cs:1040 * cs + 1040]), deps=ev_t + [tsp], dma=f"vsp{cs}")
            LU[("vsp", cs)] = tsp

        tsq = p.do("scalar", lambda e: e.activation(big_t[:], outh_t[:], AF.Square,
                                                    bias=0.0, scale=1.0),
                   deps=kt_evh + [rope_done[-1], LU.get("big")])
        mms = []
        for ic in range(8):
            mms.append(((lambda ic=ic: e_sb[:, 16 * ic:16 * ic + 16]),
                        (lambda ic=ic: big_t[:, 512 * ic:512 * ic + 512]),
                        [tsq] + ([t_es[-1]] if ic == 0 else []), False, PB(2)[0:16, 0:512]))
        g = mm_group(2, mms)
        trm = p.do("vector", lambda e: e.reduce_max(t16[:], PB(2)[0:16, 0:512], AX.X),
                   deps=[g, LU.get("t16")])
        bk.read(2, trm)
        tkm = p.do("vector", lambda e: e.tensor_max(kmax_c[:], kmax_c[:], t16[:]),
                   deps=[trm, t_kin])
        LU["t16"] = tkm
        LU["big"] = trm

    p.barrier()
    wa1.close()
    if cut == "A1":
        return _finish()

    wa2 = contextlib.ExitStack()
    load_brow(wa2, "bqr", "biasA2")
    tw_dq = load_w_hl(wa2, "wdq", D, DQ)
    tw_uq = load_w_hl(wa2, "wuq", DQ, D)
    tw_qr = load_w_hl(wa2, "wqr", DQ, D)

    tqm = None
    for stq in range(4):
        r0 = 512 * stq
        xnT_toks = []
        for j in range(8):
            xnT_toks.append(p.do("sync", lambda e, j=j, r0=r0: e.dma_start(
                out=xnTh_t[:, 512 * j:512 * j + 512],
                in_=xnT_hbm[128 * j:128 * j + 128, 0, r0:r0 + 512]),
                deps=[LU.get("xnT_t")], dma=f"xnT_rl{j % 4}"))
            xnT_toks.append(p.do("sync", lambda e, j=j, r0=r0: e.dma_start(
                out=xnTl_t[:, 512 * j:512 * j + 512],
                in_=xnT_hbm[128 * j:128 * j + 128, 1, r0:r0 + 512]),
                deps=[LU.get("xnT_t")], dma=f"xnT_rl{j % 4}"))
        xnT_last = xnT_toks[9::2]

        cq_evac = []
        for oc in range(4):
            bkid = 2 + (oc % 2)
            mms = []
            for ic in range(8):
                mms += p3((lambda ic=ic, oc=oc: WH("wdq", ic, 128 * oc, 128 * oc + 128)),
                          (lambda ic=ic, oc=oc: WL("wdq", ic, 128 * oc, 128 * oc + 128)),
                          (lambda ic=ic: xnTh_t[:, 512 * ic:512 * ic + 512]),
                          (lambda ic=ic: xnTl_t[:, 512 * ic:512 * ic + 512]),
                          [tw_dq[ic]] + xnT_last)
            g = mm_group(bkid, mms)
            LU["xnT_t"] = g
            tvh = p.do("scalar", lambda e, oc=oc, bkid=bkid: e.activation(
                ckvh_t[:, 512 * oc:512 * oc + 512], PB(bkid)[:, 0:512], AF.Identity,
                bias=bcol["bdq"][:, oc:oc + 1], scale=1.0),
                deps=[g, t_b["bdq"], LU.get("ckvT")])
            bk.read(bkid, tvh)
            tvl_ = p.do("vector", lambda e, oc=oc, bkid=bkid: e.scalar_tensor_tensor(
                ckvl_t[:, 512 * oc:512 * oc + 512], PB(bkid)[:, 0:512],
                bcol["bdq"][:, oc:oc + 1], ckvh_t[:, 512 * oc:512 * oc + 512],
                Op.add, Op.subtract), deps=[g, tvh])
            bk.read(bkid, tvl_)
            cq_evac.append(tvl_)
        LU["ckvT"] = cq_evac[-1]
        if cut == "A2q" and stq == 0:
            return _finish()

        rope_done = []
        for c in range(4):
            tcos, tsin = cos_sin(r0 + 128 * c)
            for oh in range(2):
                bkid = 4 + oh
                mms = []
                for cc in range(4):
                    mms += p3(
                        (lambda cc=cc, c=c: ckvh_t[:, 512 * cc + 128 * c:
                                                   512 * cc + 128 * c + 128]),
                        (lambda cc=cc, c=c: ckvl_t[:, 512 * cc + 128 * c:
                                                   512 * cc + 128 * c + 128]),
                        (lambda cc=cc, oh=oh: WH("wqr", cc, 512 * oh, 512 * oh + 512)),
                        (lambda cc=cc, oh=oh: WL("wqr", cc, 512 * oh, 512 * oh + 512)),
                        [tw_qr[cc], cq_evac[-1]])
                mms.append(((lambda: ones8[0:1, :]),
                            (lambda oh=oh: brow["bqr"][:, 512 * oh:512 * oh + 512]),
                            [t_ones, t_b["bqr"]], False, None))
                g = mm_group(bkid, mms, extra_deps=[LU.get(("krp", c, oh))])
                dst = big_t[:, 1024 * c + 512 * oh: 1024 * c + 512 * oh + 512]
                t6, readers = rope_apply(PB(bkid)[:, 0:512], oh, dst, g, tcos, tsin)
                for rt in readers + [t6]:
                    bk.read(bkid, rt)
                LU[("krp", c, oh)] = t6
                rope_done.append(t6)
        if cut == "A2r" and stq == 0:
            return _finish()

        qt_evh = []
        for oc in range(8):
            bkid = 6 + (oc % 2)
            mms = []
            for cc in range(4):
                mms += p3((lambda cc=cc, oc=oc: WH("wuq", cc, 128 * oc, 128 * oc + 128)),
                          (lambda cc=cc, oc=oc: WL("wuq", cc, 128 * oc, 128 * oc + 128)),
                          (lambda cc=cc: ckvh_t[:, 512 * cc:512 * cc + 512]),
                          (lambda cc=cc: ckvl_t[:, 512 * cc:512 * cc + 512]),
                          [tw_uq[cc], cq_evac[-1]])
            for c in range(4):
                mms.append(((lambda c=c, oc=oc: big_t[:, 1024 * c + 128 * oc:
                                                      1024 * c + 128 * oc + 128]),
                            (lambda: ident[:]),
                            [rope_done[-1], t_id], True, PB(bkid)[:, 128 * c:128 * c + 128]))
            g = mm_group(bkid, mms)
            th_ = p.do("scalar", lambda e, oc=oc, bkid=bkid: e.activation(
                outh_t[:, 512 * oc:512 * oc + 512], PB(bkid)[:, 0:512], AF.Identity,
                bias=buqs[:, oc:oc + 1], scale=INV_SCALE),
                deps=[g, t_buqs, LU.get(("outT", oc))])
            qt_evh.append(th_)
            bk.read(bkid, th_)
            tfull = p.do("scalar", lambda e, oc=oc, bkid=bkid: e.activation(
                tmpa[:], PB(bkid)[:, 0:512], AF.Identity,
                bias=buqs[:, oc:oc + 1], scale=INV_SCALE),
                deps=[g, t_buqs, LU.get("tmpa")])
            bk.read(bkid, tfull)
            tl_ = p.do("vector", lambda e, oc=oc: e.tensor_tensor(
                outl_t[:, 512 * oc:512 * oc + 512], tmpa[:],
                outh_t[:, 512 * oc:512 * oc + 512], Op.subtract), deps=[tfull, th_])
            LU["tmpa"] = tl_
            t1 = p.do("sync", lambda e, oc=oc, r0=r0: e.dma_start(
                out=qT_hbm[128 * oc:128 * oc + 128, 0, r0:r0 + 512],
                in_=outh_t[:, 512 * oc:512 * oc + 512]), deps=[th_, tl_], dma=f"osp{oc}")
            tsp = p.do("sync", lambda e, oc=oc, r0=r0: e.dma_start(
                out=qT_hbm[128 * oc:128 * oc + 128, 1, r0:r0 + 512],
                in_=outl_t[:, 512 * oc:512 * oc + 512]), deps=[th_, tl_, t1], dma=f"osp{oc}")
            LU[("outT", oc)] = tsp
        if cut == "A2s" and stq == 0:
            return _finish()

        tsq = p.do("scalar", lambda e: e.activation(big_t[:], outh_t[:], AF.Square,
                                                    bias=0.0, scale=1.0),
                   deps=qt_evh + [rope_done[-1], LU.get("big")])
        mms = []
        for ic in range(8):
            mms.append(((lambda ic=ic: e_sb[:, 16 * ic:16 * ic + 16]),
                        (lambda ic=ic: big_t[:, 512 * ic:512 * ic + 512]),
                        [tsq], False, PB(2)[0:16, 0:512]))
        g = mm_group(2, mms)
        trm = p.do("vector", lambda e: e.reduce_max(t16[:], PB(2)[0:16, 0:512], AX.X),
                   deps=[g, LU.get("t16")])
        bk.read(2, trm)
        tqm = p.do("vector", lambda e: e.tensor_max(qmax_c[:], qmax_c[:], t16[:]),
                   deps=[trm, t_qin])
        LU["t16"] = tqm
        LU["big"] = trm
        if cut == "A2x" and stq == 0:
            return _finish()

    g1 = mm_group(3, [((lambda: kmax_c[:]), (lambda: ident[0:16, 0:16]),
                       [tkm, tqm, t_id], True, PB(3)[0:1, 0:16])])
    tkr = p.do("vector", lambda e: e.tensor_copy(mrow[:, 0:16], PB(3)[0:1, 0:16]), deps=[g1])
    bk.read(3, tkr)
    g2 = mm_group(3, [((lambda: qmax_c[:]), (lambda: ident[0:16, 0:16]),
                       [tqm, t_id], True, PB(3)[0:1, 0:16])])
    tqr2 = p.do("vector", lambda e: e.tensor_copy(mrow[:, 16:32], PB(3)[0:1, 0:16]), deps=[g2])
    bk.read(3, tqr2)
    tpr = p.do("vector", lambda e: e.tensor_tensor(
        mrow[:, 0:16], mrow[:, 0:16], mrow[:, 16:32], Op.mult), deps=[tkr, tqr2])
    tm2 = p.do("vector", lambda e: e.reduce_max(mrow[:, 32:33], mrow[:, 0:16], AX.X), deps=[tpr])
    tm1 = p.do("scalar", lambda e: e.activation(mrow[:, 32:33], mrow[:, 32:33], AF.Sqrt,
                                                bias=0.0, scale=1.0), deps=[tm2])
    tmn = p.do("vector", lambda e: e.tensor_scalar_mul(mrow[:, 32:33], mrow[:, 32:33], -1.0),
               deps=[tm1])
    g3 = mm_group(3, [((lambda: ones8[0:1, :]), (lambda: mrow[:, 32:33]),
                       [tmn, t_ones], False, PB(3)[:, 0:1])])
    t_negm = p.do("vector", lambda e: e.tensor_copy(negM[:], PB(3)[:, 0:1]), deps=[g3])
    bk.read(3, t_negm)
    if d_negm is not None:
        p.do("sync", lambda e: e.dma_start(out=d_negm[:, :], in_=negM[0:1, :]),
             deps=[t_negm], dma="dump")

    p.barrier()
    wa2.close()
    astack.close()
    if cut == "A2":
        return _finish()

    cdstack = contextlib.ExitStack()
    xn2Th_sb = mk_sb(cdstack)("xn2Th_sb", [128, 8 * 2048], BF16)
    xn2Tl_sb = mk_sb(cdstack)("xn2Tl_sb", [128, 8 * 2048], BF16)
    bcstack = contextlib.ExitStack()
    OTh_sb = mk_sb(bcstack)("OTh_sb", [128, 8 * 2048], BF16)
    OTl_sb = mk_sb(bcstack)("OTl_sb", [128, 8 * 2048], BF16)
    bstack = contextlib.ExitStack()
    bsb = mk_sb(bstack)
    qt_sb = bsb("qt_sb", [128, 4 * 2048], BF16)
    kt_sb = [bsb(f"kt_sb{i}", [128, 512], BF16) for i in range(2)]
    vt_sb = [bsb(f"vt_sb{i}", [128, 2 * 260], BF16) for i in range(2)]
    pt32 = bsb("pt32", [128, 2 * 512])
    pth = bsb("pth", [128, 8 * 512], BF16)
    ptl = bsb("ptl", [128, 8 * 512], BF16)
    o32 = [sq_scr[0:64, 512:1024]]
    lrec_row = sq_scr[0:1, 0:512]
    oh_t = [bsb("oh_t0", [64, 512], BF16)]
    ol_t = [bsb("ol_t0", [64, 512], BF16)]

    tev = None
    for Q in range(4):
        qload = []
        for hp in range(2):
            for hl in range(2):
                qload.append(p.do("sync", lambda e, Q=Q, hp=hp, hl=hl: e.dma_start(
                    out=qt_sb[:, 2048 * (2 * hp + hl):2048 * (2 * hp + hl) + 2048],
                    in_=qT_hbm[256 * Q + 128 * hp:256 * Q + 128 * hp + 128, hl, :]),
                    deps=[LU.get("qt_sb")], dma=f"qld{2 * hp + hl}"))
        for qt in range(4):
            av_tok = [None] * 4

            def emit_av(pend):
                kcp, bufip, tvlp, infos = pend
                last = None
                for hh, thv, tlv in infos:
                    s = 4 * (kcp % 2) + hh
                    vh = lambda bufip=bufip, hh=hh: vt_sb[bufip][:, 65 * hh:65 * hh + 65]
                    vl = lambda bufip=bufip, hh=hh: vt_sb[bufip][:, 260 + 65 * hh:
                                                                  260 + 65 * hh + 65]
                    ph = lambda s=s: pth[:, 512 * s:512 * s + 512]
                    pl_ = lambda s=s: ptl[:, 512 * s:512 * s + 512]
                    tav = None
                    for pi, (vf, pf, dtok) in enumerate(
                            [(vh, ph, thv), (vh, pl_, tlv), (vl, ph, thv)]):
                        dd = [dtok, tvlp]
                        if kcp == 0 and pi == 0:
                            dd = dd + bk.pre_write(hh)
                        tav = p.do("tensor", lambda e, vf=vf, pf=pf, hh=hh, kcp=kcp, pi=pi:
                                   e.matmul(
                            PB(hh)[0:65, 0:512], vf(), pf(),
                            start=(kcp == 0 and pi == 0),
                            stop=(kcp == 31 and pi == 2)), deps=dd)
                    if kcp == 0:
                        bk.wrote(hh, tav)
                    av_tok[hh] = tav
                    LU[("pt", kcp % 2, hh)] = tav
                    last = tav
                LU[("vt", bufip)] = last

            pend = None
            for kc in range(32):
                bufi = kc % 2
                tk = p.do("sync", lambda e, Q=Q, kc=kc, bufi=bufi: e.dma_start(
                    out=kt_sb[bufi][:].rearrange("p (a hl k) -> p a hl k", a=2, k=128),
                    in_=kT_hbm[256 * Q:256 * Q + 256, kc, :, :]
                        .rearrange("(a p) hl k -> p a hl k", p=128)),
                    deps=[LU.get(("kt", bufi))], dma=f"kt{bufi}")
                tvl = p.do("sync", lambda e, Q=Q, kc=kc, bufi=bufi: e.dma_start(
                    out=vt_sb[bufi][:].rearrange("p (hl c) -> p hl c", hl=2),
                    in_=v_hbm[128 * kc:128 * kc + 128, :, 260 * Q:260 * Q + 260]),
                    deps=[LU.get(("vt", bufi))], dma=f"vt{bufi}")
                sc_tok = [None] * 4
                for hp in range(2):
                    sb0 = 4 + 2 * hp
                    sb1 = 5 + 2 * hp
                    kh = lambda bufi=bufi, hp=hp: kt_sb[bufi][:, 256 * hp:256 * hp + 128]
                    kl = lambda bufi=bufi, hp=hp: kt_sb[bufi][:, 256 * hp + 128:
                                                              256 * hp + 256]
                    qh = lambda hp=hp, qt=qt: qt_sb[:, 4096 * hp + 512 * qt:
                                                    4096 * hp + 512 * qt + 512]
                    ql = lambda hp=hp, qt=qt: qt_sb[:, 4096 * hp + 2048 + 512 * qt:
                                                    4096 * hp + 2048 + 512 * qt + 512]
                    deps_s = [tk] + qload
                    tA = tB = None
                    for pi, (kf, qf) in enumerate([(kh, qh), (kh, ql), (kl, qh)]):
                        tA = p.do("tensor", lambda e, kf=kf, qf=qf, sb0=sb0, pi=pi: e.matmul(
                            PB(sb0)[:, 0:512], kf()[0:64, :], qf()[0:64, :],
                            start=(pi == 0), stop=(pi == 2)),
                            deps=deps_s + (bk.pre_write(sb0) if pi == 0 else []))
                    for pi, (kf, qf) in enumerate([(kh, qh), (kh, ql), (kl, qh)]):
                        tB = p.do("tensor", lambda e, kf=kf, qf=qf, sb1=sb1, pi=pi: e.matmul(
                            PB(sb1)[:, 0:512], kf()[64:128, :], qf()[64:128, :],
                            start=(pi == 0), stop=(pi == 2), tile_position=(64, 0)),
                            deps=deps_s + (bk.pre_write(sb1) if pi == 0 else []))
                    bk.wrote(sb0, tA)
                    bk.wrote(sb1, tB)
                    sc_tok[2 * hp] = tA
                    sc_tok[2 * hp + 1] = tB
                LU[("kt", bufi)] = sc_tok[3]
                if pend is not None:
                    emit_av(pend)
                infos = []
                for hh in range(4):
                    sbb = 4 + hh
                    s = 4 * (kc % 2) + hh
                    s32 = hh % 2
                    tex = p.do("scalar", lambda e, sbb=sbb, s32=s32: e.activation(
                        pt32[:, 512 * s32:512 * s32 + 512], PB(sbb)[:, 0:512],
                        AF.Exp, bias=negM[:], scale=1.0),
                        deps=[sc_tok[hh], t_negm, LU.get(("pt32", s32))])
                    bk.read(sbb, tex)
                    thv = p.do("vector", lambda e, s=s, s32=s32: e.tensor_copy(
                        pth[:, 512 * s:512 * s + 512],
                        pt32[:, 512 * s32:512 * s32 + 512]),
                        deps=[tex, LU.get(("pt", kc % 2, hh))])
                    tlv = p.do("vector", lambda e, s=s, s32=s32: e.tensor_tensor(
                        ptl[:, 512 * s:512 * s + 512],
                        pt32[:, 512 * s32:512 * s32 + 512],
                        pth[:, 512 * s:512 * s + 512], Op.subtract),
                        deps=[thv])
                    LU[("pt32", s32)] = tlv
                    infos.append((hh, thv, tlv))
                pend = (kc, bufi, tvl, infos)
            emit_av(pend)
            for hh in range(4):
                g = 4 * Q + hh
                fc = g // 2
                prow = 64 * (g % 2)
                trc = p.do("vector", lambda e, hh=hh: e.reciprocal(
                    lrec_row, PB(hh)[64:65, 0:512]),
                    deps=[av_tok[hh], LU.get("lrec_row")])
                bk.read(hh, trc)
                gbc = mm_group(7, [((lambda: ones8[0:1, 0:64]), (lambda: lrec_row),
                                    [trc, t_ones], False, PB(7)[0:64, 0:512])])
                obuf = o32[0]
                tcp = p.do("vector", lambda e, hh=hh, obuf=obuf: e.tensor_copy(
                    obuf, PB(hh)[0:64, 0:512]),
                    deps=[av_tok[hh], LU.get("o32")])
                bk.read(hh, tcp)
                tnm = p.do("vector", lambda e, obuf=obuf: e.tensor_tensor(
                    obuf, obuf, PB(7)[0:64, 0:512], Op.mult), deps=[tcp, gbc])
                bk.read(7, tnm)
                LU["lrec_row"] = tnm
                ohb, olb = oh_t[0], ol_t[0]
                tch = p.do("vector", lambda e, obuf=obuf, ohb=ohb: e.tensor_copy(
                    ohb[:], obuf), deps=[tnm, LU.get("ohl")])
                tcl = p.do("vector", lambda e, obuf=obuf, ohb=ohb, olb=olb:
                           e.scalar_tensor_tensor(
                    olb[:], obuf, 1.0, ohb[:], Op.mult, Op.subtract), deps=[tch])
                LU["o32"] = tcl
                t1 = p.do("gpsimd", lambda e, fc=fc, prow=prow, qt=qt, ohb=ohb: e.dma_start(
                    out=OTh_sb[prow:prow + 64,
                               2048 * fc + 512 * qt: 2048 * fc + 512 * qt + 512],
                    in_=ohb[:]), deps=[tcl], dma="oev0")
                tev = p.do("gpsimd", lambda e, fc=fc, prow=prow, qt=qt, olb=olb: e.dma_start(
                    out=OTl_sb[prow:prow + 64,
                               2048 * fc + 512 * qt: 2048 * fc + 512 * qt + 512],
                    in_=olb[:]), deps=[tcl, t1], dma="oev1")
                LU["ohl"] = tev
        LU["qt_sb"] = tev

    if d_OT is not None:
        for j in range(8):
            p.do("gpsimd", lambda e, j=j: e.dma_start(
                out=d_OT[128 * j:128 * j + 128, :],
                in_=OTh_sb[:, 2048 * j:2048 * j + 2048]), deps=[tev], dma="dumpg")

    p.barrier()
    bstack.close()
    if cut == "B":
        return _finish()

    wcstack = contextlib.ExitStack()
    load_brow(wcstack, "bo", "biasC")
    tw_o = load_w_hl(wcstack, "wo", D, D)
    ao_b = mk_sb(wcstack)("ao_b", [128, 1024])

    def x16(t):
        return t.rearrange("p (jj c2 k) -> p jj c2 k", c2=16, k=128)

    for qc in range(16):
        aa = ao_t if qc % 2 == 0 else ao_b
        ao_toks = []
        for oh in range(2):
            bkid = oh
            mms = []
            for dc in range(8):
                mms += p3((lambda dc=dc, qc=qc: OTh_sb[:, 2048 * dc + 128 * qc:
                                                       2048 * dc + 128 * qc + 128]),
                          (lambda dc=dc, qc=qc: OTl_sb[:, 2048 * dc + 128 * qc:
                                                       2048 * dc + 128 * qc + 128]),
                          (lambda dc=dc, oh=oh: WH("wo", dc, 512 * oh, 512 * oh + 512)),
                          (lambda dc=dc, oh=oh: WL("wo", dc, 512 * oh, 512 * oh + 512)),
                          [tw_o[-1]])
            mms.append(((lambda: ones8[0:1, :]),
                        (lambda oh=oh: brow["bo"][:, 512 * oh:512 * oh + 512]),
                        [t_ones, t_b["bo"]], False, None))
            g = mm_group(bkid, mms)
            tv = p.do("scalar", lambda e, oh=oh, bkid=bkid, aa=aa: e.activation(
                aa[:, 512 * oh:512 * oh + 512], PB(bkid)[:, 0:512], AF.Identity,
                bias=0.0, scale=1.0), deps=[g, LU.get(("aoC", qc % 2))])
            bk.read(bkid, tv)
            ao_toks.append(tv)
        ts = ln_stats(ao_toks[-1], aa[:], dep_extra=[ao_toks[0]])
        tln = p.do("scalar", lambda e, aa=aa: e.activation(
            lnout[:], aa[:], AF.Identity, bias=ln_m[:], scale=ln_c[3][:]),
            deps=[ts, ao_toks[0], ao_toks[1], LU.get("lnout")])
        LU[("aoC", qc % 2)] = tln
        tl = p.do("sync", lambda e, qc=qc: e.dma_start(
            out=xt[0][:], in_=xv[128 * qc:128 * qc + 128, :]),
            deps=[LU.get(("xt", 0))], dma="xt0")
        tres = p.do("vector", lambda e: e.tensor_tensor(res_t[:], xt[0][:], lnout[:], Op.add),
                    deps=[tl, tln, LU.get("res_t"), LU.get("res_dump"),
                          LU.get("res_sp_last")])
        LU[("xt", 0)] = tres
        LU["lnout"] = tres
        LU["res_sp_last"] = p.do("sync", lambda e, qc=qc: e.dma_start(
            out=res_hbm[128 * qc:128 * qc + 128, :], in_=res_t[:]), deps=[tres], dma="res_sp")
        if d_res is not None:
            LU["res_dump"] = p.do("sync", lambda e, qc=qc: e.dma_start(
                out=d_res[128 * qc:128 * qc + 128, :], in_=res_t[:]), deps=[tres], dma="dpr")
        ts2 = ln_stats(tres, res_t[:])
        txn2 = p.do("scalar", lambda e: e.activation(
            xn[:], res_t[:], AF.Identity, bias=ln_m[:], scale=ln_c[3][:]),
            deps=[ts2, LU.get("xn")])
        LU["res_t"] = txn2
        ev_last = None
        for half in range(2):
            bkid = 2 + half
            mms = []
            for jj in range(4):
                j = 4 * half + jj
                mms.append(((lambda j=j: xn[:, 128 * j:128 * j + 128]),
                            (lambda: ident[:]),
                            [txn2, t_id], True, PB(bkid)[:, 128 * jj:128 * jj + 128]))
            g = mm_group(bkid, mms)
            th_ = p.do("vector", lambda e, bkid=bkid, half=half, qc=qc: e.tensor_copy(
                x16(xn2Th_sb)[:, 4 * half:4 * half + 4, qc, :],
                PB(bkid)[:, 0:512].rearrange("p (j k) -> p j k", k=128)),
                deps=[g])
            bk.read(bkid, th_)
            tl_ = p.do("vector", lambda e, bkid=bkid, half=half, qc=qc:
                       e.scalar_tensor_tensor(
                x16(xn2Tl_sb)[:, 4 * half:4 * half + 4, qc, :],
                PB(bkid)[:, 0:512].rearrange("p (j k) -> p j k", k=128), 1.0,
                x16(xn2Th_sb)[:, 4 * half:4 * half + 4, qc, :],
                Op.mult, Op.subtract), deps=[g, th_])
            bk.read(bkid, tl_)
            ev_last = tl_
        LU["xn"] = ev_last

    p.barrier()
    wcstack.close()
    bcstack.close()
    if cut == "C":
        return _finish()

    dstack = contextlib.ExitStack()
    dsb = mk_sb(dstack)
    y_acc = dsb("y_acc", [128, 16 * 1024])
    rstack = contextlib.ExitStack()
    rsb = mk_sb(rstack)
    wr_raw = rsb("wr_raw", [8, 1024])
    wrTh_sb = rsb("wrTh_sb", [128, 8 * 8], BF16)
    wrTl_sb = rsb("wrTl_sb", [128, 8 * 8], BF16)
    ez_sb = rsb("ez_sb", [128, 8])
    zcol = rsb("zcol", [128, 1])

    twr = p.do("gpsimd", lambda e: e.dma_start(out=wr_raw[:], in_=win["wr"][:, :]), dma="wr")
    t_wrT = None
    for ic in range(8):
        g = mm_group(4, [((lambda ic=ic: wr_raw[0:8, 128 * ic:128 * ic + 128]),
                          (lambda: ident[0:8, 0:8]),
                          [twr, t_id], True, PB(4)[:, 0:8])])
        th_ = p.do("vector", lambda e, ic=ic: e.tensor_copy(
            wrTh_sb[:, 8 * ic:8 * ic + 8], PB(4)[:, 0:8]), deps=[g])
        bk.read(4, th_)
        t_wrT = p.do("vector", lambda e, ic=ic: e.scalar_tensor_tensor(
            wrTl_sb[:, 8 * ic:8 * ic + 8], PB(4)[:, 0:8], 1.0,
            wrTh_sb[:, 8 * ic:8 * ic + 8], Op.mult, Op.subtract), deps=[g, th_])
        bk.read(4, t_wrT)

    for qc in range(16):
        mms = []
        for ic in range(8):
            for lf, rf in [
                    ((lambda ic=ic, qc=qc: xn2Th_sb[:, 2048 * ic + 128 * qc:
                                                    2048 * ic + 128 * qc + 128]),
                     (lambda ic=ic: wrTh_sb[:, 8 * ic:8 * ic + 8])),
                    ((lambda ic=ic, qc=qc: xn2Th_sb[:, 2048 * ic + 128 * qc:
                                                    2048 * ic + 128 * qc + 128]),
                     (lambda ic=ic: wrTl_sb[:, 8 * ic:8 * ic + 8])),
                    ((lambda ic=ic, qc=qc: xn2Tl_sb[:, 2048 * ic + 128 * qc:
                                                    2048 * ic + 128 * qc + 128]),
                     (lambda ic=ic: wrTh_sb[:, 8 * ic:8 * ic + 8]))]:
                mms.append((lf, rf, [t_wrT], False, PB(4)[:, 0:8]))
        g = mm_group(4, mms)
        tmx = p.do("vector", lambda e: e.reduce_max(zcol[:], PB(4)[:, 0:8], AX.X, negate=True),
                   deps=[g, LU.get("zcol")])
        bk.read(4, tmx)
        tex = p.do("scalar", lambda e: e.activation(
            ez_sb[:], PB(4)[:, 0:8], AF.Exp, bias=zcol[:], scale=1.0, accum_out=zcol[:]),
            deps=[tmx, LU.get("ez")])
        bk.read(4, tex)
        trc = p.do("vector", lambda e: e.reciprocal(zcol[:], zcol[:]), deps=[tex])
        tp7 = p.do("vector", lambda e, qc=qc: e.tensor_tensor(
            p7_sb[:, qc:qc + 1], ez_sb[:, 7:8], zcol[:], Op.mult), deps=[trc])
        LU["zcol"] = tp7
        LU["ez"] = tp7

    p.barrier()
    rstack.close()
    w1h_sb = [dsb(f"w1h_sb{i}", [128, 8 * 256], BF16) for i in range(2)]
    w1l_sb = [dsb(f"w1l_sb{i}", [128, 8 * 256], BF16) for i in range(2)]
    w2h_sb = [dsb(f"w2h_sb{i}", [128, 2 * 1024], BF16) for i in range(2)]
    w2l_sb = [dsb(f"w2l_sb{i}", [128, 2 * 1024], BF16) for i in range(2)]
    h32 = ao_t[:, 0:512]
    hh_sb = dsb("hh_sb", [128, 4 * 512], BF16)
    hl_sb = dsb("hl_sb", [128, 4 * 512], BF16)
    load_brow(dstack, "b2", "biasD")

    tz = p.do("vector", lambda e: e.memset(y_acc[:], 0.0))
    yacc_tok = {qc: tz for qc in range(16)}

    for f in range(16):
        bufi = f % 2
        tw1 = []
        for ic in range(8):
            par = ic % 2
            tc = p.do("gpsimd", lambda e, f=f, ic=ic, bufi=bufi: e.dma_start(
                out=w1h_sb[bufi][:, 256 * ic:256 * ic + 256],
                in_=win["w1"][128 * ic:128 * ic + 128, 256 * f:256 * f + 256]),
                deps=[LU.get(("wcast", par)), LU.get(("w1", bufi))], dma=f"wc{par}")
            tf = p.do("sync", lambda e, f=f, ic=ic, par=par: e.dma_start(
                out=wscr[par][:, 0:256],
                in_=win["w1"][128 * ic:128 * ic + 128, 256 * f:256 * f + 256]),
                deps=[LU.get(("wscr", par)), LU.get(("w1", bufi))], dma=f"wf{par}")
            tt = p.do("vector", lambda e, ic=ic, bufi=bufi, par=par: e.scalar_tensor_tensor(
                w1l_sb[bufi][:, 256 * ic:256 * ic + 256], wscr[par][:, 0:256], 1.0,
                w1h_sb[bufi][:, 256 * ic:256 * ic + 256], Op.mult, Op.subtract),
                deps=[tc, tf])
            LU[("wcast", par)] = tt
            LU[("wscr", par)] = tt
            tw1.append(tt)
        tw2 = []
        for fs in range(2):
            par = fs
            tc = p.do("gpsimd", lambda e, f=f, fs=fs, bufi=bufi: e.dma_start(
                out=w2h_sb[bufi][:, 1024 * fs:1024 * fs + 1024],
                in_=win["w2"][256 * f + 128 * fs:256 * f + 128 * fs + 128, :]),
                deps=[LU.get(("wcast", par)), LU.get(("w2", bufi))], dma=f"wc{par}")
            tf = p.do("sync", lambda e, f=f, fs=fs, par=par: e.dma_start(
                out=wscr[par][:, 0:1024],
                in_=win["w2"][256 * f + 128 * fs:256 * f + 128 * fs + 128, :]),
                deps=[LU.get(("wscr", par)), LU.get(("w2", bufi))], dma=f"wf{par}")
            tt = p.do("vector", lambda e, fs=fs, bufi=bufi, par=par: e.scalar_tensor_tensor(
                w2l_sb[bufi][:, 1024 * fs:1024 * fs + 1024], wscr[par][:, 0:1024], 1.0,
                w2h_sb[bufi][:, 1024 * fs:1024 * fs + 1024], Op.mult, Op.subtract),
                deps=[tc, tf])
            LU[("wcast", par)] = tt
            LU[("wscr", par)] = tt
            tw2.append(tt)
        ylast = None

        def emit_w2(pendw):
            qtp, h_toksp = pendw
            slp = qtp % 2
            yl = None
            for qs in range(4):
                qc = 4 * qtp + qs
                for oh in range(2):
                    bkid = 2 * (qs % 2) + oh
                    mms = []
                    for fs in range(2):
                        mms += p3(
                            (lambda fs=fs, qs=qs, slp=slp: hh_sb[
                                :, 1024 * slp + 512 * fs + 128 * qs:
                                1024 * slp + 512 * fs + 128 * qs + 128]),
                            (lambda fs=fs, qs=qs, slp=slp: hl_sb[
                                :, 1024 * slp + 512 * fs + 128 * qs:
                                1024 * slp + 512 * fs + 128 * qs + 128]),
                            (lambda fs=fs, oh=oh, bufi=bufi: w2h_sb[bufi][
                                :, 1024 * fs + 512 * oh: 1024 * fs + 512 * oh + 512]),
                            (lambda fs=fs, oh=oh, bufi=bufi: w2l_sb[bufi][
                                :, 1024 * fs + 512 * oh: 1024 * fs + 512 * oh + 512]),
                            [h_toksp[fs], tw2[-1]])
                    if f == 15:
                        mms.append(((lambda: ones8[0:1, :]),
                                    (lambda oh=oh: brow["b2"][:, 512 * oh:512 * oh + 512]),
                                    [t_ones, t_b["b2"]], False, None))
                    g = mm_group(bkid, mms)
                    ty = p.do("vector", lambda e, qc=qc, oh=oh, bkid=bkid: e.tensor_tensor(
                        y_acc[:, 1024 * qc + 512 * oh: 1024 * qc + 512 * oh + 512],
                        y_acc[:, 1024 * qc + 512 * oh: 1024 * qc + 512 * oh + 512],
                        PB(bkid)[:, 0:512], Op.add),
                        deps=[g, yacc_tok[qc]])
                    bk.read(bkid, ty)
                    yacc_tok[qc] = ty
                    yl = ty
            LU[("h", slp)] = yl
            return yl

        pendw = None
        for qt in range(4):
            h_toks = []
            sl = qt % 2
            for fs in range(2):
                bkid = 4 + fs
                mms = []
                for ic in range(8):
                    mms += p3(
                        (lambda ic=ic, fs=fs, bufi=bufi: w1h_sb[bufi][
                            :, 256 * ic + 128 * fs: 256 * ic + 128 * fs + 128]),
                        (lambda ic=ic, fs=fs, bufi=bufi: w1l_sb[bufi][
                            :, 256 * ic + 128 * fs: 256 * ic + 128 * fs + 128]),
                        (lambda ic=ic, qt=qt: xn2Th_sb[:, 2048 * ic + 512 * qt:
                                                       2048 * ic + 512 * qt + 512]),
                        (lambda ic=ic, qt=qt: xn2Tl_sb[:, 2048 * ic + 512 * qt:
                                                       2048 * ic + 512 * qt + 512]),
                        [tw1[-1]])
                g = mm_group(bkid, mms)
                tg = p.do("scalar", lambda e, fs=fs, f=f, bkid=bkid: e.activation(
                    h32, PB(bkid)[:, 0:512], AF.Gelu,
                    bias=bcol["b1"][:, 2 * f + fs:2 * f + fs + 1], scale=1.0),
                    deps=[g, t_b["b1"], LU.get("h32")])
                bk.read(bkid, tg)
                thc = p.do("scalar", lambda e, fs=fs, sl=sl: e.copy(
                    hh_sb[:, 1024 * sl + 512 * fs:1024 * sl + 512 * fs + 512], h32),
                    deps=[tg, LU.get(("h", sl))])
                tlc = p.do("vector", lambda e, fs=fs, sl=sl: e.scalar_tensor_tensor(
                    hl_sb[:, 1024 * sl + 512 * fs:1024 * sl + 512 * fs + 512], h32, 1.0,
                    hh_sb[:, 1024 * sl + 512 * fs:1024 * sl + 512 * fs + 512],
                    Op.mult, Op.subtract),
                    deps=[thc])
                LU["h32"] = tlc
                h_toks.append(tlc)
            if pendw is not None:
                ylast = emit_w2(pendw)
            pendw = (qt, h_toks)
        ylast = emit_w2(pendw)
        LU[("w1", bufi)] = ylast
        LU[("w2", bufi)] = ylast

    LU["ao"] = LU.get("h32")
    for qc in range(16):
        tyb = p.do("vector", lambda e, qc=qc: e.tensor_scalar(
            ao_t[:], y_acc[:, 1024 * qc:1024 * qc + 1024], p7_sb[:, qc:qc + 1], None, Op.mult),
            deps=[yacc_tok[qc], LU.get("ao")])
        LU["ao"] = tyb
        ts = ln_stats(tyb, ao_t[:])
        tln = p.do("scalar", lambda e: e.activation(
            lnout[:], ao_t[:], AF.Identity, bias=ln_m[:], scale=ln_c[3][:]),
            deps=[ts, tyb, LU.get("lnout")])
        LU["ao"] = tln
        trl = p.do("sync", lambda e, qc=qc: e.dma_start(
            out=res_t[:], in_=res_hbm[128 * qc:128 * qc + 128, :]),
            deps=[LU.get("res_t"), LU.get("out_last")], dma="res_rl")
        tout = p.do("vector", lambda e: e.tensor_tensor(res_t[:], res_t[:], lnout[:], Op.add),
                    deps=[trl, tln])
        LU["res_t"] = tout
        LU["lnout"] = tout
        LU["out_last"] = p.do("sync", lambda e, qc=qc: e.dma_start(
            out=out_hbm[128 * qc:128 * qc + 128, :], in_=res_t[:]), deps=[tout], dma="out")

    dstack.close()
    cdstack.close()

    return _finish()


_cache = {}


def kernel(**kw):
    x = np.asarray(kw["L0_attn_x"], np.float32)
    if "nc" not in _cache:
        _cache["nc"] = build_program()
    nc = _cache["nc"]
    shared = {
        "wdq": kw["L0_attn_Wd_q_w"], "wuq": kw["L0_attn_Wu_q_w"],
        "wqr": kw["L0_attn_Wq_r_w"], "wkr": kw["L0_attn_Wk_r_w"],
        "wdkv": kw["L0_attn_Wd_kv_w"], "wuk": kw["L0_attn_Wu_k_w"],
        "wuv": kw["L0_attn_Wu_v_w"], "wo": kw["L0_attn_Wo_w"],
        "wr": kw["L0_ffn_Wr"], "w1": kw["L0_ffn_W1"][7], "w2": kw["L0_ffn_W2"][7],
        "bdq": kw["L0_attn_Wd_q_b"], "buq": kw["L0_attn_Wu_q_b"],
        "bqr": kw["L0_attn_Wq_r_b"], "bkr": kw["L0_attn_Wk_r_b"],
        "bdkv": kw["L0_attn_Wd_kv_b"], "buk": kw["L0_attn_Wu_k_b"],
        "buv": kw["L0_attn_Wu_v_b"], "bo": kw["L0_attn_Wo_b"],
        "b1": kw["L0_ffn_b1"][7], "b2": kw["L0_ffn_b2"][7],
    }
    shared = {k: np.ascontiguousarray(np.asarray(v, np.float32)) for k, v in shared.items()}
    in_maps = []
    for core in range(8):
        b, hf = core // 2, core % 2
        q0 = NQ * hf
        xvv = np.ascontiguousarray(np.concatenate([x[b, q0:], x[b, :q0]], axis=0))
        m = dict(shared)
        m["xv"] = xvv
        m["seg"] = np.array([q0, (q0 + NQ) % S], np.float32)
        in_maps.append(m)
    res = run_bass_kernel_spmd(nc, in_maps, core_ids=list(range(8)))
    r0 = np.asarray(res.results[0]["out"])
    out = np.empty((B, S, D), r0.dtype)
    for core in range(8):
        b, hf = core // 2, core % 2
        out[b, NQ * hf:NQ * hf + NQ] = np.asarray(res.results[core]["out"])
    return out
